# version 33
# baseline (speedup 1.0000x reference)
import numpy as np
import ml_dtypes
import concourse.bass as bass
import concourse.mybir as mybir
from concourse.bass_utils import run_bass_kernel_spmd

F32 = mybir.dt.float32
BF16 = mybir.dt.bfloat16
I32 = mybir.dt.int32
U8 = mybir.dt.uint8
OP = mybir.AluOpType
ACT = mybir.ActivationFunctionType
AX = mybir.AxisListType.X
BFnp = ml_dtypes.bfloat16

D = 1024
OPROJ = 3072
HID = 4096
KS = 128
TILE = 128
NHALO = 2
NT = 16
NALL = NHALO + NT
NW1T = HID // TILE
NW2T = D // TILE
NWELEM = float(HID * D)

_TAPS = ()
_LOCKSTEP = False

_TRIG_NAMES = ("wf1a", "wf1b", "wf2a", "wf2b", "winv1", "nwinv1", "winv2",
               "mc", "mi", "mb")


def _phase_pair(phase_int, sgn):
    P = np.mod(phase_int, 256).astype(np.int64)
    S = np.asarray(sgn, np.float64).copy()
    quarter = (P % 64 == 0)
    v0 = np.select([P == 0, P == 64, P == 128, P == 192], [0.0, 1.0, 0.0, -1.0])
    S = np.where(quarter, S * v0, S)
    P = np.where(quarter, 64, P)
    return P.astype(np.float32), S.astype(np.float32)


def _host_consts():
    N = 256
    t = np.arange(N)[:, None]
    f = np.arange(128)[None, :]
    c = {}

    def put(name, phase, sgn):
        P, S = _phase_pair(phase, np.broadcast_to(np.asarray(sgn, np.float64),
                                                  P0.shape))
        c["P_" + name] = P
        c["SGN_" + name] = S

    P0 = np.zeros((128, 128))
    put("wf1a", t[:128] * f + 64, 1.0)
    put("wf1b", t[128:] * f + 64, 1.0)
    ph = t * f
    ph[:, 0:1] = 128 * t + 64
    sg = np.full((N, 128), -1.0)
    sg[:, 0] = 1.0
    put("wf2a", ph[:128], sg[:128])
    put("wf2b", ph[128:], sg[128:])
    fr = np.arange(128)[:, None]
    j = np.arange(128, 256)[None, :]
    ph1 = fr * j + 64
    sg1 = np.where(fr == 0, 1.0 / N, 2.0 / N) * np.ones_like(j, dtype=np.float64)
    put("winv1", ph1, sg1)
    put("nwinv1", ph1, -sg1)
    ph2 = fr * j
    ph2[0:1, :] = 128 * j + 64
    sg2 = np.full((128, 128), -2.0 / N)
    sg2[0, :] = 1.0 / N
    put("winv2", ph2, sg2)
    kk = np.arange(KS)[:, None]
    put("mc", kk * f + 64, 1.0)
    phi = kk * f
    sgi = np.full((KS, 128), -1.0)
    sgi[:, 0] = 0.0
    put("mi", phi, sgi)
    phb = kk * f + 64
    phb[:, 0:1] = 128 * kk + 64
    put("mb", phb, 1.0)

    c["ident_bf"] = np.eye(128, dtype=BFnp)
    c["ident_f32"] = np.eye(128, dtype=np.float32)
    c["ones_row_bf"] = np.ones((1, 128), BFnp)
    c["ones_row_f32"] = np.ones((1, 128), np.float32)
    c["ones_col_f32"] = np.ones((128, 1), np.float32)
    c["ones_sq_f32"] = np.ones((128, 128), np.float32)
    return c

_CONST_SPECS = (
    [("P_" + n, [128, 128], F32, F32) for n in _TRIG_NAMES]
    + [("SGN_" + n, [128, 128], F32, F32) for n in _TRIG_NAMES]
    + [
        ("ident_bf", [128, 128], BF16, BF16), ("ident_f32", [128, 128], F32, F32),
        ("ones_row_bf", [1, 128], BF16, BF16), ("ones_row_f32", [1, 128], F32, F32),
        ("ones_col_f32", [128, 1], F32, F32), ("ones_sq_f32", [128, 128], F32, F32),
    ])
_TRIG_SBUF_DT = {"mc": F32, "mi": F32, "mb": F32}

_ENGS = ("tensor", "vector", "scalar", "gpsimd", "sync")
_DMA_LANES = {"DS": 8, "DQ": 4}
_SEMS = (("T", "V", "S", "Q")
         + tuple(f"DS{i}" for i in range(_DMA_LANES["DS"]))
         + tuple(f"DQ{i}" for i in range(_DMA_LANES["DQ"])))
_OWNER = {"T": "tensor", "V": "vector", "S": "scalar", "Q": "gpsimd"}


class Sch:
    def __init__(self):
        self.ops = {e: [] for e in _ENGS}
        self.cnt = {s: 0 for s in _SEMS}
        self.dma_rr = {"DS": 0, "DQ": 0}

    @staticmethod
    def _flat(deps):
        out = []
        def rec(d):
            if not d:
                return
            if isinstance(d, tuple) and len(d) == 2 and isinstance(d[0], str):
                out.append(d)
                return
            for x in d:
                rec(x)
        rec(tuple(deps))
        return tuple(out)

    def op(self, eng, fn, deps=(), inc=None):
        deps = self._flat(deps)
        if _LOCKSTEP:
            deps = self.barrier()
            if inc is None:
                inc = {"tensor": "T", "vector": "V", "scalar": "S",
                       "gpsimd": "Q", "sync": None}[eng]
        tok = None
        if inc:
            if inc in _DMA_LANES:
                lane = self.dma_rr[inc]
                self.dma_rr[inc] = (lane + 1) % _DMA_LANES[inc]
                sem = f"{inc}{lane}"
                if self.cnt[sem] > 0:
                    deps = deps + ((sem, self.cnt[sem]),)
                self.cnt[sem] += 16
                tok = (sem, self.cnt[sem])
                self.ops[eng].append((fn, deps, sem))
                return tok
            self.cnt[inc] += 1
            tok = (inc, self.cnt[inc])
        self.ops[eng].append((fn, deps, inc))
        return tok

    def barrier(self):
        return tuple((s, v) for s, v in self.cnt.items() if v > 0)

    def replay(self, eng_name, e, sems):
        waited = {}
        for fn, deps, inc in self.ops[eng_name]:
            for s, v in deps:
                if _OWNER.get(s) == eng_name:
                    continue
                if waited.get(s, 0) < v:
                    e.wait_ge(sems[s], v)
                    waited[s] = v
            inst = fn(e)
            if inc:
                assert inst is not None
                if inc.startswith("D"):
                    inst.then_inc(sems[inc], 16)
                else:
                    e.maybe_drain_then_inc((sems[inc], 1), fusable=True)


class Arena:

    def __init__(self, nc):
        self.nc = nc
        self.base = (nc.sbuf_base + 31) // 32 * 32
        self.total = nc.sbuf_top - self.base
        self.cur = 0
        self.marks = []

    def alloc(self, name, shape, dtype):
        per_part = int(np.prod(shape[1:])) * mybir.dt.size(dtype)
        off = (self.cur + 31) // 32 * 32
        self.cur = off + per_part
        assert self.cur <= self.total, \
            f"SBUF arena overflow at {name}: {self.cur} > {self.total}"
        return self.nc.alloc_sbuf_tensor_at(name, shape, dtype,
                                            offset=self.base + off)

    def push(self):
        self.marks.append(self.cur)

    def pop(self):
        self.cur = self.marks.pop()


def build_nc():
    nc = bass.Bass("TRN2", target_bir_lowering=False)

    dram = {}
    def din(name, shape, dtype=F32):
        dram[name] = nc.dram_tensor(name, shape, dtype, kind="ExternalInput").ap()
        return dram[name]

    x_own = din("x_own", [NT * TILE, D])
    x_halo = din("x_halo", [NHALO * TILE, D])
    hm = din("hm", [128, NHALO])
    W_in = din("W_in", [D, OPROJ])
    b_in = din("b_in", [1, OPROJ])
    k0_t = din("k0_t", [D, KS])
    k1_t = din("k1_t", [D, KS])
    gamma_t = din("gamma_t", [128, 8])
    beta_t = din("beta_t", [128, 8])
    W_out = din("W_out", [D, D])
    b_out = din("b_out", [1, D])
    nw1_r = din("nw1_r", [1, D])
    w1 = din("w1", [HID, D])
    as_i = din("as_i", [1, 1])
    ab_i = din("ab_i", [1, 1])
    nw2_r = din("nw2_r", [1, HID])
    w2 = din("w2", [D, HID])
    for cn, cs, _sd, cdd in _CONST_SPECS:
        din("c_" + cn, cs, cdd)

    out_d = nc.dram_tensor("out", [NT * TILE, D], F32, kind="ExternalOutput").ap()
    taps_d = {}
    for tname, _bk, tshape, tdt in _TAPS:
        taps_d[tname] = nc.dram_tensor(tname, tshape, tdt, kind="ExternalOutput").ap()

    sc_xp = nc.dram_tensor("sc_xp", [NT * TILE, D], BF16, kind="Internal").ap()
    sc_wq1 = nc.dram_tensor("sc_wq1", [HID, D], BF16, kind="Internal").ap()
    sc_wq2 = nc.dram_tensor("sc_wq2", [D, HID], BF16, kind="Internal").ap()

    ar = Arena(nc)
    A = ar.alloc

    cst = {}
    for cn, cs, csd, _dd in _CONST_SPECS:
        if cn.startswith("P_") or cn.startswith("SGN_"):
            continue
        cst[cn] = A("s_" + cn, cs, csd)
    for tn in _TRIG_NAMES:
        cst[tn] = A("s_" + tn, [128, 128], _TRIG_SBUF_DT.get(tn, BF16))
    hm_s = A("hm_s", [128, NHALO], F32)
    epsc = A("epsc", [128, 2], F32)
    asab = A("asab", [128, 2], F32)
    mc1b = A("mc1b", [128, 1], F32)
    mc2b = A("mc2b", [128, 1], F32)
    thr_row = A("thr_row", [1, 8], F32)
    wsum1 = A("wsum1", [128, NW1T], F32)
    wsum2 = A("wsum2", [128, NW2T], F32)
    wtot = A("wtot", [128, 2], F32)
    stat = {n: A("st_" + n, [128, 20], F32) for n in
            ("mu1", "m21", "rs1", "nmr1", "mu2", "m22", "rs2", "nmr2",
             "ssq", "amx1", "m1", "k1", "ss2", "amx2", "m2", "k2",
             "mu3", "m23", "rs3", "nmr3", "tmpa", "tmpb", "tmpc")}
    xp_t = [A(f"xp{i}", [128, D], BF16) for i in range(3)]
    out_t = A("outb", [128, D], F32)

    ar.push()
    Wg = A("Wg", [128, 8 * OPROJ], BF16)
    Wo = A("Wo", [128, 8 * D], BF16)
    Ktab = {k_: A("Ktab_" + k_, [128, D], BF16)
            for k_ in ("K0C", "K0I", "K0B", "K1C", "K1I", "K1B")}
    b_st = A("b_st", [1, OPROJ + D], BF16)
    b_eff = A("b_eff", [1, OPROJ], BF16)
    xr = [A(f"xr{i}", [128, D], F32) for i in range(4)]
    xn = [A(f"xn{i}", [128, D], BF16) for i in range(2)]
    xnT = [A(f"xnT{i}", [128, D], BF16) for i in range(2)]
    sq_sink = A("sq_sink", [128, D], F32)
    r_buf = A("r_buf", [128, D], F32)
    ar.push()
    wf32 = [A(f"wf32{i}", [128, OPROJ], F32) for i in range(2)]
    kT_b = A("kT_b", [128, D], F32)
    ar.pop()
    proj = [A(f"proj{i}", [128, OPROJ], BF16) for i in range(3)]
    y1 = [A(f"y1_{i}", [128, D], BF16) for i in range(3)]
    y2 = [A("y2_0", [128, D], BF16)]
    y2T = [A("y2T0", [128, D], BF16)]
    Pb = [A(f"Pb{i}", [128, 1024], BF16) for i in range(4)]
    a_peak = ar.cur
    ar.pop()

    ar.push()
    wqf = [A(f"wqf{i}", [128, HID], F32) for i in range(2)]
    wqb = [A(f"wqb{i}", [128, HID], BF16) for i in range(2)]
    wgt = A("wgt", [128, HID], BF16)
    totb = A("totb", [128, 1], F32)
    thrw = A("thrw", [128, 1], F32)
    nthrw = A("nthrw", [128, 1], F32)
    w_peak = ar.cur
    ar.pop()

    ar.push()
    wq1T = [A(f"wq1T{i}", [128, HID], BF16) for i in range(8)]
    wq2Tc = [A(f"wq2T{i}", [128, D], BF16) for i in range(32)]
    nw1r = A("nw1r", [128, D], BF16)
    nw2r = A("nw2r", [128, HID], BF16)
    t1 = A("t1", [128, D], F32)
    xq1 = A("xq1", [128, D], BF16)
    xq1T = A("xq1T", [128, D], BF16)
    r1s = A("r1s", [128, 1024], F32)
    t2f = A("t2f", [128, HID], F32)
    it2s = [A(f"it2s{i}", [128, 1024], I32) for i in range(2)]
    f2s = [A("f2s0", [128, 1024], BF16)]
    g2s = [A("g2s0", [128, 1024], BF16)]
    d2s = [A("d2s0", [128, 1024], BF16)]
    xq2o = [A(f"xq2o{i}", [128, 1024], BF16) for i in range(3)]
    xq2Tc = [A(f"xq2Tc{i}", [128, 1024], BF16) for i in range(2)]
    b_peak = ar.cur
    ar.pop()

    sch = Sch()

    psum_specs = [("ps_tr", [128, 1024], BF16), ("pj", [128, 1024], F32),
                  ("yc1", [128, 1024], F32), ("yc2", [128, 1024], F32),
                  ("ps_g", [128, 512], F32)]

    from contextlib import ExitStack
    ctx = ExitStack()
    ps = {n: ctx.enter_context(nc.psum_tensor(n, s, d)) for n, s, d in psum_specs}
    sems = {s: ctx.enter_context(nc.semaphore("sem_" + s)) for s in _SEMS}

    for _sn, _st in stat.items():
        sch.op("vector", (lambda st_: lambda e: e.memset(st_[:], 0.0))(_st), (), "V")
    tok_eps = sch.op("vector", lambda e: e.memset(epsc[:, 0:1], 1e-5), (), "V")
    tok_eps = sch.op("vector", lambda e: e.memset(epsc[:, 1:2], 1e-8), (tok_eps,), "V")

    free = {}
    tap_tok = {}

    def F(key):
        a, b = free.get(key), tap_tok.get(key)
        if b is None:
            return a
        return (a, b) if a is not None else b

    def pshalf(name, h):
        return ps[name][:, h * 512:(h + 1) * 512]

    def Fb(name):
        return (F(name + "0"), F(name + "1"))

    def setfree(name, tok):
        free[name + "0"] = tok
        free[name + "1"] = tok

    tok_cst = {}
    for cn, _s, csd, cdd in _CONST_SPECS:
        if cn.startswith("P_") or cn.startswith("SGN_"):
            continue
        if csd != cdd:
            tok_cst[cn] = sch.op("gpsimd", (lambda cn: lambda e: e.dma_start(
                out=cst[cn][:], in_=dram["c_" + cn]))(cn), (), "DQ")
        else:
            tok_cst[cn] = sch.op("sync", (lambda cn: lambda e: e.dma_start(
                out=cst[cn][:], in_=dram["c_" + cn]))(cn), (), "DS")

    _tp = 2.0 * np.pi
    _chi = float(np.float32(_tp / 256.0))
    _clo = float(np.float32(_tp / 256.0 - np.float64(_chi)))
    _thi = float(np.float32(_tp))
    _tlo = float(np.float32(_tp - np.float64(_thi)))
    _sin_set = None
    _main_set = None
    from concourse.hw_specs import get_activation_tables
    for _i, _funcs in enumerate(get_activation_tables(nc.m.arch).values()):
        fs = set(_funcs)
        if _sin_set is None and ACT.Sin in fs:
            _sin_set = _i
        if _main_set is None and {ACT.Identity, ACT.Square, ACT.Relu,
                                  ACT.Ln, ACT.Exp}.issubset(fs):
            _main_set = _i
    assert _sin_set is not None and _main_set is not None
    sch.op("scalar", (lambda sid: lambda e: e.add_instruction(
        mybir.InstLoadActFuncSet(
            name=nc.get_next_instruction_name(), act_func_set_id=sid,
            ins=[], outs=[])))(_sin_set), (), None)
    stage = wf32[0]
    sP, sG, sA, sB, sC = (stage[:, 0:128], stage[:, 128:256], stage[:, 256:384],
                          stage[:, 384:512], stage[:, 512:640])
    prev_gen = None
    for tn in _TRIG_NAMES:
        tokP = sch.op("sync", (lambda tn: lambda e: e.dma_start(
            out=sP, in_=dram["c_P_" + tn]))(tn), (prev_gen,), "DS")
        tokS = sch.op("sync", (lambda tn: lambda e: e.dma_start(
            out=sG, in_=dram["c_SGN_" + tn]))(tn), (prev_gen,), "DS")
        tg1 = sch.op("vector", (lambda: lambda e: e.tensor_scalar(
            sA, sP, _chi, None, OP.mult))(), (tokP,), "V")
        tg2 = sch.op("vector", (lambda: lambda e: e.scalar_tensor_tensor(
            sB, sP, _clo, sA, OP.mult, OP.add))(), (tg1,), "V")
        tg3 = sch.op("vector", (lambda: lambda e: e.tensor_scalar(
            sA, sP, 128.5, None, OP.is_gt))(), (tg2,), "V")
        tg4 = sch.op("vector", (lambda: lambda e: e.scalar_tensor_tensor(
            sC, sA, -_thi, sB, OP.mult, OP.add))(), (tg3,), "V")
        tg5 = sch.op("vector", (lambda: lambda e: e.scalar_tensor_tensor(
            sB, sA, -_tlo, sC, OP.mult, OP.add))(), (tg4,), "V")
        tg6 = sch.op("scalar", (lambda: lambda e: e.activation(
            sC, sB, ACT.Sin))(), (tg5,), "S")
        prev_gen = sch.op("vector", (lambda tn: lambda e: e.tensor_tensor(
            cst[tn][:], sC, sG, OP.mult))(tn), (tg6, tokS), "V")
        tok_cst[tn] = prev_gen
    sch.op("scalar", (lambda sid: lambda e: e.add_instruction(
        mybir.InstLoadActFuncSet(
            name=nc.get_next_instruction_name(), act_func_set_id=sid,
            ins=[], outs=[])))(_main_set), (prev_gen,), None)
    tok_hm = sch.op("sync", lambda e: e.dma_start(out=hm_s[:], in_=hm), (), "DS")
    tok_bin = sch.op("sync", lambda e: e.dma_start(
        out=thr_row[0:1, 0:1], in_=as_i), (), "DS")
    tok_abin = sch.op("sync", lambda e: e.dma_start(
        out=thr_row[0:1, 1:2], in_=ab_i), (), "DS")
    tok_ab_mm = sch.op("tensor", lambda e: e.matmul(
        ps["ps_g"][:, 0:2], cst["ones_row_f32"][:], thr_row[0:1, 0:2],
        start=True, stop=True), (tok_bin, tok_abin, tok_cst["ones_row_f32"]), "T")
    tok_asab = sch.op("vector", lambda e: e.tensor_copy(asab[:], ps["ps_g"][:, 0:2]),
                      (tok_ab_mm,), "V")
    free["ps_g"] = tok_asab

    tok_binrow = sch.op("gpsimd", lambda e: e.dma_start(
        out=b_st[0:1, 0:OPROJ], in_=b_in), (), "DQ")
    tok_boutrow = sch.op("gpsimd", lambda e: e.dma_start(
        out=b_st[0:1, OPROJ:OPROJ + D], in_=b_out), (), "DQ")

    tok_k = {}
    for ki, kd in ((0, k0_t), (1, k1_t)):
        tok_kload = sch.op("sync", (lambda kd: lambda e: e.dma_start(
            out=wf32[0][:, 0:8 * KS].rearrange("p (c k) -> p c k", c=8),
            in_=kd.rearrange("(c p) k -> p c k", p=128)))(kd),
            (F("wf32_0"),) if ki else (prev_gen,), "DS")
        tk_tr = None
        for c in range(8):
            tk_tr = sch.op("tensor", (lambda c: lambda e: e.transpose(
                ps["yc1"][:, c * 128:(c + 1) * 128], wf32[0][:, c * KS:(c + 1) * KS],
                cst["ident_f32"][:]))(c),
                (tok_kload, tok_cst["ident_f32"],
                 Fb("yc1") if c == 0 else None), "T")
        tok_kT = sch.op("vector", lambda e: e.tensor_copy(
            kT_b[:], ps["yc1"][:]), (tk_tr,), "V")
        setfree("yc1", tok_kT)
        free["wf32_0"] = tok_kT
        for tab, mat in (("C", "mc"), ("I", "mi"), ("B", "mb")):
            key = f"K{ki}{tab}"
            tmm = None
            for dc in range(2):
                tmm = sch.op("tensor", (lambda mat, dc: lambda e: e.matmul(
                    pshalf("pj", dc), cst[mat][:], kT_b[:, dc * 512:(dc + 1) * 512],
                    start=True, stop=True))(mat, dc),
                    (tok_kT, tok_cst[mat], F(f"pj{dc}")), "T" if dc == 1 else None)
            tev = sch.op("vector", (lambda key: lambda e: e.tensor_copy(
                Ktab[key][:], ps["pj"][:]))(key), (tmm,), "V")
            setfree("pj", tev)
        tok_k[ki] = tev

    bias_banks = [("pj", 0), ("pj", 1), ("yc1", 0), ("yc1", 1),
                  ("yc2", 0), ("yc2", 1)]
    gamma_s = stat["tmpa"]
    beta_s = stat["tmpb"]
    tok_gl = sch.op("sync", lambda e: e.dma_start(out=gamma_s[:, 0:8], in_=gamma_t), (), "DS")
    tok_bl = sch.op("sync", lambda e: e.dma_start(out=beta_s[:, 0:8], in_=beta_t), (), "DS")

    tok_wg_ev = {}
    tok_bias_mm = {}
    prev_wdma = [(), ()]
    for c in range(8):
        buf = c % 2
        tok_wdma = sch.op("sync", (lambda c, buf: lambda e: e.dma_start(
            out=wf32[buf][:], in_=W_in[c * 128:(c + 1) * 128, :]))(c, buf),
            prev_wdma[buf] + ((tok_k[1],) if c < 2 else ()), "DS")
        tok_wg_ev[c] = sch.op("vector", (lambda c, buf: lambda e: e.tensor_scalar(
            Wg[:, c * OPROJ:(c + 1) * OPROJ], wf32[buf][:],
            gamma_s[:, c:c + 1], None, OP.mult))(c, buf), (tok_wdma, tok_gl), "V")
        for oc in range(6):
            bn, bh = bias_banks[oc]
            tok_bias_mm[oc] = sch.op("tensor", (lambda c, buf, oc, bn, bh: lambda e:
                e.matmul(pshalf(bn, bh)[0:1, :], beta_s[:, c:c + 1],
                         wf32[buf][:, oc * 512:(oc + 1) * 512],
                         start=(c == 0), stop=(c == 7)))(c, buf, oc, bn, bh),
                (tok_wdma, tok_bl, F(f"{bn}{bh}") if c == 0 else None),
                "T" if oc == 5 else None)
        prev_wdma[buf] = (tok_wg_ev[c], tok_bias_mm[5])
    tok_beff = {}
    for oc in range(6):
        bn, bh = bias_banks[oc]
        tok_beff[oc] = sch.op("vector", (lambda oc, bn, bh: lambda e: e.tensor_tensor(
            b_eff[0:1, oc * 512:(oc + 1) * 512], pshalf(bn, bh)[0:1, :],
            b_st[0:1, oc * 512:(oc + 1) * 512], OP.add))(oc, bn, bh),
            (tok_bias_mm[5], tok_binrow), "V")
        free[f"{bn}{bh}"] = tok_beff[oc]
    tok_beff_all = tok_beff[5]

    tok_wo_all = []
    for c in range(8):
        tok_wo_all.append(sch.op("gpsimd", (lambda c: lambda e: e.dma_start(
            out=Wo[:, c * D:(c + 1) * D], in_=W_out[c * 128:(c + 1) * 128, :]))(c),
            (), "DQ"))
    tok_wo_all = tuple(tok_wo_all)

    conv_cst = tuple(tok_cst[n] for n in
                     ("wf1a", "wf1b", "wf2a", "wf2b", "winv1", "nwinv1", "winv2"))

    def emit_ln_stats(j, srcbuf, pre, eps, deps):
        mu, m2m, rs_, nmr = (stat["mu" + pre], stat["m2" + pre],
                             stat["rs" + pre], stat["nmr" + pre])
        tmu = sch.op("vector", (lambda: lambda e: e.reduce_sum(
            mu[:, j:j + 1], srcbuf[:], axis=AX))(), deps, "V")
        tm2 = sch.op("scalar", (lambda: lambda e: e.activation(
            sq_sink[:, 0:srcbuf.shape[-1]] if False else sq_sink[:],
            srcbuf[:], ACT.Square, accum_out=m2m[:, j:j + 1]))(), deps, "S")
        tmua = sch.op("scalar", (lambda: lambda e: e.activation(
            mu[:, j:j + 1], mu[:, j:j + 1], ACT.Identity, scale=1.0 / D))(), (tmu,), "S")
        tm2a = sch.op("scalar", (lambda: lambda e: e.activation(
            m2m[:, j:j + 1], m2m[:, j:j + 1], ACT.Identity, scale=1.0 / D,
            bias=epsc[:, 0:1]))(), (tm2, tok_eps), "S")
        tmu2 = sch.op("scalar", (lambda: lambda e: e.activation(
            stat["tmpc"][:, j:j + 1], mu[:, j:j + 1], ACT.Square))(), (tmua,), "S")
        tvar = sch.op("vector", (lambda: lambda e: e.tensor_tensor(
            m2m[:, j:j + 1], m2m[:, j:j + 1], stat["tmpc"][:, j:j + 1], OP.subtract))(),
            (tm2a, tmu2), "V")
        trr = sch.op("scalar", (lambda: lambda e: e.activation(
            rs_[:, j:j + 1], m2m[:, j:j + 1], ACT.Ln))(), (tvar,), "S")
        trs = sch.op("scalar", (lambda: lambda e: e.activation(
            rs_[:, j:j + 1], rs_[:, j:j + 1], ACT.Exp, scale=-0.5))(), (trr,), "S")
        tnmr = sch.op("vector", (lambda: lambda e: e.scalar_tensor_tensor(
            nmr[:, j:j + 1], mu[:, j:j + 1], -1.0, rs_[:, j:j + 1],
            OP.mult, OP.mult))(), (trs, tmua), "V")
        return trs, tnmr

    tok_x = [None] * NALL
    tok_xn = [None] * NALL
    tok_proj = [None] * NALL
    tok_y1 = [None] * NALL
    tok_y2 = [None] * NALL
    tok_r = [None] * NALL
    tok_xp_dma = [None] * NALL

    def tap(name, key_buf, deps, freekey=None):
        if not _TAPS:
            return
        for tname, bk, _s, _d in _TAPS:
            if bk == name:
                tk = sch.op("sync", (lambda tname, key_buf: lambda e: e.dma_start(
                    out=taps_d[tname], in_=key_buf[:]))(tname, key_buf), deps, "DS")
                if freekey:
                    p = tap_tok.get(freekey)
                    tap_tok[freekey] = tk if p is None else (p, tk)

    _PROJ_BANKS = ("pj", "yc1", "yc2")

    def emit_xload(j):
        xslot = j % 4
        src = x_halo[j * TILE:(j + 1) * TILE, :] if j < NHALO else \
            x_own[(j - NHALO) * TILE:(j - NHALO + 1) * TILE, :]
        tok_x[j] = sch.op("gpsimd", (lambda src, xslot: lambda e: e.dma_start(
            out=xr[xslot][:], in_=src))(src, xslot), (F(f"xr{xslot}"),), "DQ")

    def emit_ln1(j):
        xslot = j % 4
        trs, tnmr = emit_ln_stats(j, xr[xslot], "1", 1e-5, (tok_x[j],))
        xnb = j % 2
        tok_xn[j] = sch.op("scalar", (lambda j, xslot, xnb: lambda e: e.activation(
            xn[xnb][:], xr[xslot][:], ACT.Identity,
            bias=stat["nmr1"][:, j:j + 1], scale=stat["rs1"][:, j:j + 1]))(j, xslot, xnb),
            (trs, tnmr, F(f"xn{xnb}")), "S")
        if j < NHALO:
            free[f"xr{xslot}"] = tok_xn[j]

    def emit_front(j):
        xnb = j % 2
        ttr = None
        for c in range(8):
            ttr = sch.op("tensor", (lambda c, xnb: lambda e: e.transpose(
                ps["ps_tr"][:, c * 128:(c + 1) * 128],
                xn[xnb][:, c * 128:(c + 1) * 128], cst["ident_bf"][:]))(c, xnb),
                (tok_xn[j], F("ps_tr") if c == 0 else None,
                 tok_cst["ident_bf"] if j == 0 and c == 0 else None),
                "T" if c == 7 else None)
        tevt = sch.op("vector", (lambda xnb: lambda e: e.tensor_copy(
            xnT[xnb][:], ps["ps_tr"][:]))(xnb),
            (ttr, F(f"xnT{xnb}")), "V")
        free["ps_tr"] = tevt
        free[f"xn{xnb}"] = ttr
        trn = [tevt] * 8
        pslot = j % 3
        ev_last = None
        tmm = None
        for p in range(3):
            bankT = _PROJ_BANKS[p]
            for h in range(2):
                oc = 2 * p + h
                for c in range(8):
                    tmm = sch.op("tensor", (lambda oc, c, h, xnb, bankT: lambda e:
                        e.matmul(pshalf(bankT, h), xnT[xnb][:, c * 128:(c + 1) * 128],
                                 Wg[:, c * OPROJ + oc * 512: c * OPROJ + oc * 512 + 512],
                                 start=(c == 0), stop=False))(oc, c, h, xnb, bankT),
                        (trn[c], F(f"{bankT}{h}") if c == 0 else None,
                         tok_wg_ev[c] if j == 0 else None),
                        None)
                tmm = sch.op("tensor", (lambda oc, h, bankT: lambda e: e.matmul(
                    pshalf(bankT, h), cst["ones_row_bf"][:],
                    b_eff[0:1, oc * 512:oc * 512 + 512],
                    start=False, stop=True))(oc, h, bankT),
                    (tok_beff_all if j == 0 and p == 0 else None,),
                    "T" if h == 1 else None)
            if j < NHALO and p == 0:
                tev = sch.op("vector", (lambda j, p, pslot, bankT: lambda e:
                    e.tensor_scalar(proj[pslot][:, p * 1024:(p + 1) * 1024],
                                    ps[bankT][:], hm_s[:, j:j + 1], None,
                                    OP.mult))(j, p, pslot, bankT),
                    (tmm, tok_hm, F(f"proj{pslot}")), "V")
            else:
                tev = sch.op("vector", (lambda p, pslot, bankT: lambda e:
                    e.tensor_copy(proj[pslot][:, p * 1024:(p + 1) * 1024],
                                  ps[bankT][:]))(p, pslot, bankT),
                    (tmm, F(f"proj{pslot}")), "V")
            setfree(bankT, tev)
            ev_last = tev
        tok_proj[j] = ev_last
        free[f"xnT{xnb}"] = tmm

    def conv_hop(jj, src_bufs, src_toks, gate_buf, gate_off, gate_tok,
                 Kpre, dst_buf, dst_free_key, extra=()):
        tf1 = tf2 = None
        for dc in range(2):
            tf1 = sch.op("tensor", (lambda dc: lambda e: e.matmul(
                pshalf("yc1", dc), cst["wf1a"][:],
                src_bufs[0][:, dc * 512:(dc + 1) * 512],
                start=True, stop=False))(dc),
                (src_toks[0], F(f"yc1{dc}")) + (extra if dc == 0 else ()), None)
            tf1 = sch.op("tensor", (lambda dc: lambda e: e.matmul(
                pshalf("yc1", dc), cst["wf1b"][:],
                src_bufs[1][:, dc * 512:(dc + 1) * 512],
                start=False, stop=True))(dc), (src_toks[1],),
                "T" if dc == 1 else None)
        for dc in range(2):
            tf2 = sch.op("tensor", (lambda dc: lambda e: e.matmul(
                pshalf("yc2", dc), cst["wf2a"][:],
                src_bufs[0][:, dc * 512:(dc + 1) * 512],
                start=True, stop=False))(dc), (F(f"yc2{dc}"),), None)
            tf2 = sch.op("tensor", (lambda dc: lambda e: e.matmul(
                pshalf("yc2", dc), cst["wf2b"][:],
                src_bufs[1][:, dc * 512:(dc + 1) * 512],
                start=False, stop=True))(dc), (),
                "T" if dc == 1 else None)
        tp = []
        for pi, (bank, tabk) in enumerate(
                (("yc1", Kpre + "C"), ("yc1", Kpre + "I"),
                 ("yc2", Kpre + "I"), ("yc2", Kpre + "B"))):
            tdep = tf1 if bank == "yc1" else tf2
            tp.append(sch.op("vector", (lambda pi, bank, tabk: lambda e:
                e.tensor_tensor(Pb[pi][:], ps[bank][:], Ktab[tabk][:],
                                OP.mult))(pi, bank, tabk),
                (tdep, F(f"Pb{pi}")), "V"))
        setfree("yc1", tp[1])
        setfree("yc2", tp[3])
        tiv = None
        for dc in range(2):
            for mi_, (wkey, pi) in enumerate(
                    (("winv1", 0), ("nwinv1", 2), ("winv2", 1), ("winv2", 3))):
                tiv = sch.op("tensor", (lambda wkey, pi, dc, mi_: lambda e: e.matmul(
                    pshalf("pj", dc), cst[wkey][:],
                    Pb[pi][:, dc * 512:(dc + 1) * 512],
                    start=(mi_ == 0), stop=(mi_ == 3)))(wkey, pi, dc, mi_),
                    (tp[pi], F(f"pj{dc}")) if mi_ == 0 else (tp[pi],),
                    "T" if (dc == 1 and mi_ == 3) else None)
        for pi in range(4):
            free[f"Pb{pi}"] = tiv
        tg = sch.op("vector", (lambda gate_off: lambda e: e.tensor_tensor(
            dst_buf[:], ps["pj"][:],
            gate_buf[:, gate_off:gate_off + 1024], OP.mult))(gate_off),
            (tiv, gate_tok, F(dst_free_key)), "V")
        setfree("pj", tg)
        return tg

    def emit_conv1(j):
        pprev, pcur = (j - 1) % 3, j % 3
        y1slot = j % 3
        tok_y1[j] = conv_hop(
            j, (proj[pprev], proj[pcur]), (tok_proj[j - 1], tok_proj[j]),
            proj[pcur], D, tok_proj[j], "K0", y1[y1slot], f"y1_{y1slot}",
            extra=conv_cst if j == 1 else ())

    def emit_conv2_wout(j):
        pcur = j % 3
        xslot = j % 4
        tok_y2[j] = conv_hop(
            j, (y1[(j - 1) % 3], y1[j % 3]), (tok_y1[j - 1], tok_y1[j]),
            proj[pcur], 2 * D, tok_proj[j], "K1", y2[0], "y2_0")
        free[f"y1_{(j - 1) % 3}"] = tok_y2[j]
        free[f"proj{pcur}"] = (free.get(f"proj{pcur}"), tok_y2[j])
        ttr2 = None
        for c in range(8):
            ttr2 = sch.op("tensor", (lambda c: lambda e: e.transpose(
                ps["ps_tr"][:, c * 128:(c + 1) * 128],
                y2[0][:, c * 128:(c + 1) * 128], cst["ident_bf"][:]))(c),
                (tok_y2[j], F("ps_tr") if c == 0 else None),
                "T" if c == 7 else None)
        tevt2 = sch.op("vector", (lambda: lambda e: e.tensor_copy(
            y2T[0][:], ps["ps_tr"][:]))(), (ttr2, F("y2T0")), "V")
        free["ps_tr"] = tevt2
        free["y2_0"] = ttr2
        trn2 = [tevt2] * 8
        tmm2 = None
        for oc in range(2):
            for c in range(8):
                tmm2 = sch.op("tensor", (lambda oc, c: lambda e: e.matmul(
                    pshalf("pj", oc), y2T[0][:, c * 128:(c + 1) * 128],
                    Wo[:, c * D + oc * 512: c * D + oc * 512 + 512],
                    start=(c == 0), stop=False))(oc, c),
                    (trn2[c], F(f"pj{oc}") if c == 0 else None)
                    + (tok_wo_all if j == 2 and oc == 0 and c == 0 else ()),
                    None)
            tmm2 = sch.op("tensor", (lambda oc: lambda e: e.matmul(
                pshalf("pj", oc), cst["ones_row_bf"][:],
                b_st[0:1, OPROJ + oc * 512:OPROJ + oc * 512 + 512],
                start=False, stop=True))(oc),
                (tok_boutrow if j == 2 and oc == 0 else None,),
                "T" if oc == 1 else None)
        ev_r = sch.op("vector", (lambda xslot: lambda e:
            e.scalar_tensor_tensor(r_buf[:], xr[xslot][:], 2.0,
                                   ps["pj"][:], OP.mult, OP.add))(xslot),
            (tmm2, tok_x[j], F("r_buf")), "V")
        setfree("pj", ev_r)
        tok_r[j] = ev_r
        free["y2T0"] = tmm2
        free[f"xr{xslot}"] = ev_r
        trs2, tnmr2 = emit_ln_stats(j, r_buf, "2", 1e-5, (tok_r[j],))
        xpb = j % 2
        tok_xp = sch.op("scalar", (lambda j, xpb: lambda e: e.activation(
            xp_t[xpb][:], r_buf[:], ACT.Identity,
            bias=stat["nmr2"][:, j:j + 1], scale=stat["rs2"][:, j:j + 1]))(j, xpb),
            (trs2, tnmr2, F(f"xp{xpb}")), "S")
        free["r_buf"] = tok_xp
        tok_xp_dma[j] = sch.op("gpsimd", (lambda j, xpb: lambda e: e.dma_start(
            out=sc_xp[(j - NHALO) * TILE:(j - NHALO + 1) * TILE, :],
            in_=xp_t[xpb][:]))(j, xpb), (tok_xp,), "DQ")
        free[f"xp{xpb}"] = tok_xp_dma[j]

    emit_xload(0)
    emit_ln1(0)
    emit_xload(1)
    for i in range(NALL + 2):
        if i < NALL:
            emit_front(i)
        if 1 <= i - 1 < NALL:
            emit_conv1(i - 1)
        if i + 1 < NALL:
            emit_ln1(i + 1)
        if 2 <= i - 2 < NALL:
            emit_conv2_wout(i - 2)
        if i + 2 < NALL:
            emit_xload(i + 2)

    barrierA = sch.barrier()

    def wquant(wdram, ntiles, width, wsum, scdram, mc_t, wtcol):
        toks_sum = []
        prev = [F("wqf0"), F("wqf1")]
        for i in range(ntiles):
            buf = i % 2
            td = sch.op("gpsimd", (lambda i, buf: lambda e: e.dma_start(
                out=wqf[buf][:, 0:width], in_=wdram[i * 128:(i + 1) * 128, :]))(i, buf),
                barrierA + ((prev[buf],) if prev[buf] else ()), "DQ")
            ts_ = sch.op("vector", (lambda i, buf: lambda e: e.reduce_sum(
                wsum[:, i:i + 1], wqf[buf][:, 0:width], axis=AX,
                apply_absolute_value=True))(i, buf), (td,), "V")
            prev[buf] = ts_
            toks_sum.append(ts_)
        trow = sch.op("vector", (lambda: lambda e: e.reduce_sum(
            wtot[:, wtcol:wtcol + 1], wsum[:, 0:ntiles], axis=AX))(),
            tuple(toks_sum), "V")
        tap(f"wsum_{wtcol}", wsum, (trow,))
        tap(f"wtot_{wtcol}", wtot, (trow,), freekey="wtot")
        tmm = sch.op("tensor", (lambda: lambda e: e.matmul(
            ps["ps_g"][:, 0:1], cst["ones_sq_f32"][:], wtot[:, wtcol:wtcol + 1],
            start=True, stop=True))(), (trow, F("ps_g"), tok_cst["ones_sq_f32"]), "T")
        tev = sch.op("vector", (lambda: lambda e: e.tensor_copy(
            totb[:], ps["ps_g"][:, 0:1]))(), (tmm, F("totb")), "V")
        free["ps_g"] = tev
        tap(f"totb_{wtcol}", totb, (tev,), freekey="totb")
        tmc = sch.op("vector", (lambda: lambda e: e.tensor_scalar(
            mc_t[:], totb[:], 1.0 / NWELEM, 1e-5, OP.mult, OP.max))(), (tev,), "V")
        free["totb"] = tmc
        tth = sch.op("vector", (lambda: lambda e: e.tensor_scalar(
            thrw[:], mc_t[:], 0.5, None, OP.mult))(), (tmc, F("thrw")), "V")
        tnt = sch.op("vector", (lambda: lambda e: e.tensor_scalar(
            nthrw[:], thrw[:], -1.0, None, OP.mult))(), (tth, F("nthrw")), "V")
        prev_d = [None, None]
        tlast = None
        tstores = []
        for i in range(ntiles):
            buf = i % 2
            td = sch.op("gpsimd", (lambda i, buf: lambda e: e.dma_start(
                out=wqf[buf][:, 0:width], in_=wdram[i * 128:(i + 1) * 128, :]))(i, buf),
                ((prev_d[buf],) if prev_d[buf] else (toks_sum[-1],)), "DQ")
            tg_ = sch.op("vector", (lambda i, buf: lambda e: e.tensor_scalar(
                wgt[:, 0:width], wqf[buf][:, 0:width], nthrw[:], None,
                OP.is_lt))(i, buf), (td, tnt, F("wgt")), "V")
            tq_ = sch.op("vector", (lambda i, buf: lambda e: e.scalar_tensor_tensor(
                wqb[buf][:, 0:width], wqf[buf][:, 0:width], thrw[:],
                wgt[:, 0:width], OP.is_gt, OP.subtract))(i, buf),
                (tg_, F(f"wqb{buf}")), "V")
            free["wgt"] = tq_
            prev_d[buf] = tq_
            tst = sch.op("sync", (lambda i, buf: lambda e: e.dma_start(
                out=scdram[i * 128:(i + 1) * 128, :], in_=wqb[buf][:, 0:width]))(i, buf),
                (tq_,), "DS")
            free[f"wqb{buf}"] = tst
            tlast = tst
            tstores.append(tst)
        free["thrw"] = tlast
        free["nthrw"] = tlast
        free["wqf0"] = prev_d[0]
        free["wqf1"] = prev_d[1]
        return tuple(tstores)

    tok_wq1 = wquant(w1, NW1T, D, wsum1, sc_wq1, mc1b, 0)
    tok_wq2 = wquant(w2, NW2T, HID, wsum2, sc_wq2, mc2b, 1)
    tap("mc1b", mc1b, tok_wq1)
    tap("mc2b", mc2b, tok_wq2)

    barrierW = sch.barrier()
    tok_wq1T = [None] * 8
    for dc in range(8):
        tok_wq1T[dc] = sch.op("sync", (lambda dc: lambda e: e.dma_start(
            out=wq1T[dc][:], in_=sc_wq1[:, dc * 128:(dc + 1) * 128],
            transpose=True))(dc), barrierW + tok_wq1, "DS")
    for dc in range(8):
        tap(f"wq1T{dc}", wq1T[dc], (tok_wq1T[dc],))
    tok_wq2T = [None] * 32
    for hc in range(32):
        tok_wq2T[hc] = sch.op("sync", (lambda hc: lambda e: e.dma_start(
            out=wq2Tc[hc][:], in_=sc_wq2[:, hc * 128:(hc + 1) * 128],
            transpose=True))(hc), barrierW + tok_wq2, "DS")

    tok_nw1row = sch.op("sync", lambda e: e.dma_start(
        out=t1[0:1, 0:D], in_=nw1_r), barrierW, "DS")
    tok_nw2row = sch.op("sync", lambda e: e.dma_start(
        out=t2f[0:1, 0:HID], in_=nw2_r), barrierW, "DS")
    tok_nwb = {}
    for oc in range(2):
        tm = sch.op("tensor", (lambda oc: lambda e: e.matmul(
            ps["ps_g"][:], cst["ones_row_f32"][:], t1[0:1, oc * 512:(oc + 1) * 512],
            start=True, stop=True))(oc), barrierW + (tok_nw1row, F("ps_g")), "T")
        tok_nwb[oc] = sch.op("vector", (lambda oc: lambda e: e.tensor_copy(
            nw1r[:, oc * 512:(oc + 1) * 512], ps["ps_g"][:]))(oc), (tm,), "V")
        free["ps_g"] = tok_nwb[oc]
    tok_nw1r = tok_nwb[1]
    tok_nw1r_T = tm
    for oc in range(8):
        tm = sch.op("tensor", (lambda oc: lambda e: e.matmul(
            ps["ps_g"][:], cst["ones_row_f32"][:], t2f[0:1, oc * 512:(oc + 1) * 512],
            start=True, stop=True))(oc), (tok_nw2row, F("ps_g")), "T")
        tok_nwb[oc] = sch.op("vector", (lambda oc: lambda e: e.tensor_copy(
            nw2r[:, oc * 512:(oc + 1) * 512], ps["ps_g"][:]))(oc), (tm,), "V")
        free["ps_g"] = tok_nwb[oc]
    tok_nw2r = tok_nwb[7]
    free["t1"] = tok_nw1r_T
    for _k in range(4):
        free[f"t2f{_k}"] = tm

    def quant_round(dst_bf, src, m_col, itbuf, fbuf, gbuf, dbuf, width, deps,
                    levels=1):
        tcast = sch.op("scalar", (lambda: lambda e: e.activation(
            itbuf[:, 0:width], src[:, 0:width], ACT.Identity, scale=m_col))(),
            deps, "S")
        tf = sch.op("vector", (lambda: lambda e: e.scalar_tensor_tensor(
            fbuf[:, 0:width], src[:, 0:width], m_col, itbuf[:, 0:width],
            OP.mult, OP.subtract))(), (tcast,), "V")
        tabs = sch.op("scalar", (lambda: lambda e: e.activation(
            dbuf[:, 0:width], fbuf[:, 0:width], ACT.Abs))(), (tf,), "S")
        tsgn = sch.op("scalar", (lambda: lambda e: e.activation(
            fbuf[:, 0:width], fbuf[:, 0:width], ACT.Sign))(), (tabs,), "S")
        tn = sch.op("vector", (lambda: lambda e: e.tensor_scalar(
            gbuf[:, 0:width], dbuf[:, 0:width], 0.5, None, OP.is_gt))(), (tabs,), "V")
        for lv in range(1, levels):
            tn = sch.op("vector", (lambda lv: lambda e: e.scalar_tensor_tensor(
                gbuf[:, 0:width], dbuf[:, 0:width], 0.5 + lv, gbuf[:, 0:width],
                OP.is_gt, OP.add))(lv), (tn,), "V")
        td_ = sch.op("vector", (lambda: lambda e: e.tensor_tensor(
            dbuf[:, 0:width], fbuf[:, 0:width], gbuf[:, 0:width], OP.mult))(),
            (tn, tsgn), "V")
        tq_ = sch.op("vector", (lambda: lambda e: e.tensor_tensor(
            dst_bf[:, 0:width], itbuf[:, 0:width], dbuf[:, 0:width], OP.add))(),
            (td_,), "V")
        return tq_, tf

    def rms_quant_chain(t, pre, ss_src_tok, amax_tok, mcb, nelem):
        ssn, amx, mcol, kcol = (stat["ss" + pre] if ("ss" + pre) in stat else stat["ssq"],
                                stat["amx" + pre], stat["m" + pre], stat["k" + pre])
        ta = sch.op("scalar", (lambda: lambda e: e.activation(
            stat["tmpa"][:, t:t + 1], ssn[:, t:t + 1], ACT.Identity,
            scale=1.0 / nelem, bias=epsc[:, 1:2]))(), (ss_src_tok, tok_eps), "S")
        tb = sch.op("scalar", (lambda: lambda e: e.activation(
            stat["tmpb"][:, t:t + 1], stat["tmpa"][:, t:t + 1], ACT.Ln))(), (ta,), "S")
        trs = sch.op("scalar", (lambda: lambda e: e.activation(
            stat["tmpa"][:, t:t + 1], stat["tmpb"][:, t:t + 1], ACT.Exp,
            scale=-0.5))(), (tb,), "S")
        tc = sch.op("vector", (lambda: lambda e: e.tensor_tensor(
            stat["tmpc"][:, t:t + 1], amx[:, t:t + 1], stat["tmpa"][:, t:t + 1],
            OP.mult))(), (trs, amax_tok), "V")
        tc2 = sch.op("vector", (lambda: lambda e: e.tensor_scalar(
            stat["tmpc"][:, t:t + 1], stat["tmpc"][:, t:t + 1], 1e-5, None, OP.max))(),
            (tc,), "V")
        trc0 = sch.op("scalar", (lambda: lambda e: e.activation(
            stat["tmpb"][:, t:t + 1], stat["tmpc"][:, t:t + 1], ACT.Ln))(), (tc2,), "S")
        trc = sch.op("scalar", (lambda: lambda e: e.activation(
            stat["tmpb"][:, t:t + 1], stat["tmpb"][:, t:t + 1], ACT.Exp,
            scale=-1.0))(), (trc0,), "S")
        tm_ = sch.op("vector", (lambda: lambda e: e.scalar_tensor_tensor(
            mcol[:, t:t + 1], stat["tmpb"][:, t:t + 1], 127.0, stat["tmpa"][:, t:t + 1],
            OP.mult, OP.mult))(), (trc, F("stm" + pre)), "V")
        tk_ = sch.op("vector", (lambda: lambda e: e.scalar_tensor_tensor(
            kcol[:, t:t + 1], stat["tmpc"][:, t:t + 1], 1.0 / 127.0, mcb,
            OP.mult, OP.mult))(), (tc2, F("stk" + pre)), "V")
        return tm_, tk_

    tok_out_dma = []
    itp = [0]
    Fst = {}
    Sst = {}

    def emit_stageF(t):
        xpb = t % 3
        d = {"xpb": xpb}
        tok_xpl = sch.op("sync", (lambda t, xpb: lambda e: e.dma_start(
            out=xp_t[xpb][:], in_=sc_xp[t * TILE:(t + 1) * TILE, :]))(t, xpb),
            (F(f"xp{xpb}"), tok_xp_dma[t + NHALO]), "DS")
        tss = sch.op("scalar", (lambda t, xpb: lambda e: e.activation(
            t1[:], xp_t[xpb][:], ACT.Square,
            accum_out=stat["ssq"][:, t:t + 1]))(t, xpb),
            (tok_xpl, F("t1")), "S")
        tt1 = sch.op("vector", (lambda xpb: lambda e: e.tensor_tensor(
            t1[:], xp_t[xpb][:], nw1r[:], OP.mult))(xpb),
            (tss, tok_xpl, tok_nw1r), "V")
        tam = sch.op("vector", (lambda t: lambda e: e.reduce_max(
            stat["amx1"][:, t:t + 1], t1[:], axis=AX,
            apply_absolute_value=True))(t), (tt1,), "V")
        tm1, tk1 = rms_quant_chain(t, "1", tss, tam, mc1b[:], D)
        d["tk1"] = tk1
        tq1, _tf1 = quant_round(xq1, t1, stat["m1"][:, t:t + 1],
                                it2s[itp[0]], xq1, g2s[0], d2s[0], 1024,
                                (tm1, tt1, F("xq1"), F(f"it{itp[0]}")),
                                levels=3)
        free[f"it{itp[0]}"] = tq1
        itp[0] ^= 1
        free["t1"] = tq1
        d["tq1"] = tq1
        return d

    def emit_xq1T(d):
        tq1 = d["tq1"]
        trq1 = []
        for c in range(8):
            trq1.append(sch.op("sync", (lambda c: lambda e: e.dma_start(
                out=xq1T[:, c * 128:(c + 1) * 128],
                in_=xq1[:, c * 128:(c + 1) * 128], transpose=True))(c),
                (tq1, F("xq1T") if c == 0 else None), "DS"))
        d["trq1"] = trq1
        free["xq1"] = tuple(trq1)

    def emit_chunkSV(tp_, k):
        tq2, tf2 = quant_round(
            xq2o[k % 3], t2f[:, k * 1024:(k + 1) * 1024],
            stat["m2"][:, tp_:tp_ + 1], it2s[itp[0]], f2s[0], g2s[0],
            d2s[0], 1024,
            (Sst[tp_]["tok_t2"][k], Sst[tp_]["tm2"], F(f"it{itp[0]}"),
             F(f"xq2o{k % 3}")), levels=5)
        free[f"it{itp[0]}"] = tq2
        itp[0] ^= 1
        free[f"t2f{k}"] = tf2
        return tq2

    def emit_superstrip(t, ss):
        tmm = [None, None]
        for h in range(2):
            for dc in range(8):
                tmm[h] = sch.op("tensor",
                    (lambda ss, dc, h: lambda e: e.matmul(
                        pshalf("pj", h), xq1T[:, dc * 128:(dc + 1) * 128],
                        wq1T[dc][:, ss * 1024 + h * 512:
                                   ss * 1024 + (h + 1) * 512],
                        start=(dc == 0), stop=(dc == 7)))(ss, dc, h),
                    (Fst[t]["trq1"][dc],
                     F(f"pj{h}") if dc == 0 else None,
                     tok_wq1T[dc] if t == 0 else None),
                    "T" if dc == 7 else None)
        trel = sch.op("scalar", (lambda t: lambda e: e.activation(
            r1s[:], ps["pj"][:], ACT.Relu,
            scale=stat["k1"][:, t:t + 1]))(t),
            (tmm[0], tmm[1], Fst[t]["tk1"], F("r1s")), "S")
        setfree("pj", trel)
        tsq = sch.op("scalar", (lambda: lambda e: e.activation(
            r1s[:], r1s[:], ACT.Square))(), (trel,), "S")
        th2 = sch.op("scalar", (lambda ss: lambda e: e.activation(
            t2f[:, ss * 1024:(ss + 1) * 1024], r1s[:], ACT.Identity,
            scale=asab[:, 0:1], bias=asab[:, 1:2]))(ss),
            (tsq, tok_asab if t == 0 else None, F(f"t2f{ss}")), "S")
        free["r1s"] = th2
        if ss == 3:
            free["xq1T"] = tmm[1]
        return th2

    def emit_chunkT(tp_, k, tq2):
        kb = k % 2
        ko = k % 3
        trk = []
        for c in range(8):
            trk.append(sch.op("sync", (lambda c, kb, ko: lambda e: e.dma_start(
                out=xq2Tc[kb][:, c * 128:(c + 1) * 128],
                in_=xq2o[ko][:, c * 128:(c + 1) * 128],
                transpose=True))(c, kb, ko),
                (tq2, F(f"xq2Tc{kb}") if c == 0 else None), "DS"))
        tmm2 = None
        for oc in range(2):
            for q in range(8):
                tmm2 = sch.op("tensor",
                    (lambda k, q, oc, kb: lambda e: e.matmul(
                        pshalf("yc1", oc),
                        xq2Tc[kb][:, q * 128:(q + 1) * 128],
                        wq2Tc[k * 8 + q][:, oc * 512:(oc + 1) * 512],
                        start=(k == 0 and q == 0),
                        stop=(k == 3 and q == 7),
                        skip_group_check=True))(k, q, oc, kb),
                    (trk[q],
                     tok_wq2T[k * 8 + q] if tp_ == 0 and oc == 0 else None,
                     F(f"yc1{oc}") if (k == 0 and q == 0) else None),
                    "T" if (oc == 1 and q == 7) else None)
        free[f"xq2Tc{kb}"] = tmm2
        free[f"xq2o{ko}"] = tuple(trk)
        return tmm2

    def emit_stat_chunk(t, k, th2_tok):
        tss2 = sch.op("scalar", (lambda k: lambda e: e.activation(
            t1[:], t2f[:, k * 1024:(k + 1) * 1024], ACT.Square,
            accum_out=stat["ss2"][:, 16 + k:17 + k]))(k),
            (th2_tok, F("t1")), "S")
        tt2 = sch.op("vector", (lambda k: lambda e: e.tensor_tensor(
            t2f[:, k * 1024:(k + 1) * 1024],
            t2f[:, k * 1024:(k + 1) * 1024],
            nw2r[:, k * 1024:(k + 1) * 1024], OP.mult))(k),
            (tss2, tok_nw2r if t == 0 else None), "V")
        tam2 = sch.op("vector", (lambda k: lambda e: e.reduce_max(
            stat["amx2"][:, 16 + k:17 + k],
            t2f[:, k * 1024:(k + 1) * 1024], axis=AX,
            apply_absolute_value=True))(k), (tt2,), "V")
        return tss2, tt2, tam2

    def emit_stats(t, tok_sc):
        d = {}
        tok_ss2 = [x[0] for x in tok_sc]
        tok_t2 = [x[1] for x in tok_sc]
        tok_am2 = [x[2] for x in tok_sc]
        tssc = sch.op("vector", (lambda t: lambda e: e.reduce_sum(
            stat["ss2"][:, t:t + 1], stat["ss2"][:, 16:20], axis=AX))(t),
            tuple(tok_ss2), "V")
        tamc = sch.op("vector", (lambda t: lambda e: e.reduce_max(
            stat["amx2"][:, t:t + 1], stat["amx2"][:, 16:20], axis=AX))(t),
            tuple(tok_am2), "V")
        tm2, tk2 = rms_quant_chain(t, "2", tssc, tamc, mc2b[:], HID)
        d["tm2"], d["tk2"] = tm2, tk2
        d["tok_t2"] = tok_t2
        return d

    emit_taps = None

    Fst[0] = None
    for t in range(NT + 1):
        if t == 0:
            Fst[0] = emit_stageF(0)
            emit_xq1T(Fst[0])
        if t + 1 < NT:
            Fst[t + 1] = emit_stageF(t + 1)
        tok_sc = [None] * 4
        tq2s = {}
        tmm2_last = None
        for k in range(5):
            if t >= 1 and k < 4:
                tq2s[k] = emit_chunkSV(t - 1, k)
            if t < NT and 1 <= k <= 4:
                th2_tok = emit_superstrip(t, k - 1)
                tok_sc[k - 1] = emit_stat_chunk(t, k - 1, th2_tok)
            if t >= 1 and k >= 1:
                tmm2_last = emit_chunkT(t - 1, k - 1, tq2s[k - 1])
        if t + 1 < NT:
            emit_xq1T(Fst[t + 1])
        if t < NT:
            Sst[t] = emit_stats(t, tok_sc)
        if t >= 1:
            tp_ = t - 1
            pxpb = tp_ % 3
            ev3 = sch.op("vector", (lambda tp_, pxpb: lambda e:
                e.scalar_tensor_tensor(out_t[:], ps["yc1"][:],
                                       stat["k2"][:, tp_:tp_ + 1],
                                       xp_t[pxpb][:], OP.mult, OP.add))(tp_, pxpb),
                (tmm2_last, Sst[tp_]["tk2"], F("outb")), "V")
            setfree("yc1", ev3)
            free[f"xp{pxpb}"] = ev3
        if t >= 1:
            tp_ = t - 1
            tmu3 = sch.op("vector", (lambda tp_: lambda e: e.reduce_sum(
                stat["mu3"][:, tp_:tp_ + 1], out_t[:], axis=AX))(tp_), (ev3,), "V")
            tm23 = sch.op("scalar", (lambda tp_: lambda e: e.activation(
                t1[:], out_t[:], ACT.Square,
                accum_out=stat["m23"][:, tp_:tp_ + 1]))(tp_),
                (ev3, F("t1")), "S")
            tmua = sch.op("scalar", (lambda tp_: lambda e: e.activation(
                stat["mu3"][:, tp_:tp_ + 1], stat["mu3"][:, tp_:tp_ + 1],
                ACT.Identity, scale=1.0 / D))(tp_), (tmu3,), "S")
            tm2a = sch.op("scalar", (lambda tp_: lambda e: e.activation(
                stat["m23"][:, tp_:tp_ + 1], stat["m23"][:, tp_:tp_ + 1],
                ACT.Identity, scale=1.0 / D, bias=epsc[:, 0:1]))(tp_),
                (tm23, tok_eps), "S")
            tmu2 = sch.op("scalar", (lambda tp_: lambda e: e.activation(
                stat["tmpc"][:, tp_:tp_ + 1], stat["mu3"][:, tp_:tp_ + 1],
                ACT.Square))(tp_), (tmua,), "S")
            tvar = sch.op("vector", (lambda tp_: lambda e: e.tensor_tensor(
                stat["m23"][:, tp_:tp_ + 1], stat["m23"][:, tp_:tp_ + 1],
                stat["tmpc"][:, tp_:tp_ + 1], OP.subtract))(tp_),
                (tm2a, tmu2), "V")
            trr = sch.op("scalar", (lambda tp_: lambda e: e.activation(
                stat["rs3"][:, tp_:tp_ + 1], stat["m23"][:, tp_:tp_ + 1],
                ACT.Ln))(tp_), (tvar,), "S")
            trs3 = sch.op("scalar", (lambda tp_: lambda e: e.activation(
                stat["rs3"][:, tp_:tp_ + 1], stat["rs3"][:, tp_:tp_ + 1],
                ACT.Exp, scale=-0.5))(tp_), (trr,), "S")
            tnmr3 = sch.op("vector", (lambda tp_: lambda e: e.scalar_tensor_tensor(
                stat["nmr3"][:, tp_:tp_ + 1], stat["mu3"][:, tp_:tp_ + 1], -1.0,
                stat["rs3"][:, tp_:tp_ + 1], OP.mult, OP.mult))(tp_),
                (trs3, tmua), "V")
            tout = sch.op("scalar", (lambda tp_: lambda e: e.activation(
                out_t[:], out_t[:], ACT.Identity,
                bias=stat["nmr3"][:, tp_:tp_ + 1],
                scale=stat["rs3"][:, tp_:tp_ + 1]))(tp_), (trs3, tnmr3), "S")
            tdma = sch.op("sync", (lambda tp_: lambda e: e.dma_start(
                out=out_d[tp_ * TILE:(tp_ + 1) * TILE, :], in_=out_t[:]))(tp_),
                (tout,), "DS")
            free["outb"] = tdma
            tok_out_dma.append(tdma)

    final = sch.barrier()

    with nc.Block() as block:
        @block.sync
        def _(e):
            sch.replay("sync", e, sems)
            for s, v in final:
                e.wait_ge(sems[s], v)

        @block.tensor
        def _(e):
            sch.replay("tensor", e, sems)

        @block.vector
        def _(e):
            sch.replay("vector", e, sems)

        @block.scalar
        def _(e):
            sch.replay("scalar", e, sems)

        @block.gpsimd
        def _(e):
            sch.replay("gpsimd", e, sems)

    ctx.close()
    return nc


_cache = {}


def kernel(L0_attn_x, L0_attn_W_in, L0_attn_b_in, L0_attn_k0, L0_attn_k1,
           L0_attn_gamma, L0_attn_beta, L0_attn_W_out, L0_attn_b_out,
           L0_ffn_norm1_w, L0_ffn_w1, L0_ffn_act_scale, L0_ffn_act_bias,
           L0_ffn_norm2_w, L0_ffn_w2):
    x = np.ascontiguousarray(np.asarray(L0_attn_x, dtype=np.float32))
    B, T, _ = x.shape
    consts = _host_consts()
    nc = build_nc()

    shared = {
        "W_in": np.ascontiguousarray(np.asarray(L0_attn_W_in, np.float32)),
        "b_in": np.ascontiguousarray(np.asarray(L0_attn_b_in, np.float32)).reshape(1, OPROJ),
        "k0_t": np.ascontiguousarray(np.asarray(L0_attn_k0, np.float32)).reshape(D, KS),
        "k1_t": np.ascontiguousarray(np.asarray(L0_attn_k1, np.float32)).reshape(D, KS),
        "gamma_t": np.ascontiguousarray(
            np.asarray(L0_attn_gamma, np.float32).reshape(8, 128).T),
        "beta_t": np.ascontiguousarray(
            np.asarray(L0_attn_beta, np.float32).reshape(8, 128).T),
        "W_out": np.ascontiguousarray(np.asarray(L0_attn_W_out, np.float32)),
        "b_out": np.ascontiguousarray(np.asarray(L0_attn_b_out, np.float32)).reshape(1, D),
        "nw1_r": np.ascontiguousarray(np.asarray(L0_ffn_norm1_w, np.float32)).reshape(1, D),
        "w1": np.ascontiguousarray(np.asarray(L0_ffn_w1, np.float32)),
        "as_i": np.ascontiguousarray(np.asarray(L0_ffn_act_scale, np.float32)).reshape(1, 1),
        "ab_i": np.ascontiguousarray(np.asarray(L0_ffn_act_bias, np.float32)).reshape(1, 1),
        "nw2_r": np.ascontiguousarray(np.asarray(L0_ffn_norm2_w, np.float32)).reshape(1, HID),
        "w2": np.ascontiguousarray(np.asarray(L0_ffn_w2, np.float32)),
    }
    for cn, _s, _sd, cdd in _CONST_SPECS:
        arrn = consts[cn]
        shared["c_" + cn] = arrn

    zeros_halo = np.zeros((NHALO * TILE, D), np.float32)
    hm0 = np.zeros((128, NHALO), np.float32)
    hm1 = np.ones((128, NHALO), np.float32)

    in_maps = []
    for core in range(8):
        b, h = core // 2, core % 2
        s = h * (NT * TILE)
        m = dict(shared)
        m["x_own"] = np.ascontiguousarray(x[b, s:s + NT * TILE])
        if h == 0:
            m["x_halo"] = zeros_halo
            m["hm"] = hm0
        else:
            m["x_halo"] = np.ascontiguousarray(x[b, s - NHALO * TILE:s])
            m["hm"] = hm1
        in_maps.append(m)

    res = run_bass_kernel_spmd(nc, in_maps, core_ids=list(range(8)))
    out = np.empty((B, T, D), res.results[0]["out"].dtype)
    for core in range(8):
        b, h = core // 2, core % 2
        s = h * (NT * TILE)
        out[b, s:s + NT * TILE] = res.results[core]["out"]
    return out


# version 34
# speedup vs baseline: 1.0323x; 1.0323x over previous
import numpy as np
import ml_dtypes
import concourse.bass as bass
import concourse.mybir as mybir
from concourse.bass_utils import run_bass_kernel_spmd

F32 = mybir.dt.float32
BF16 = mybir.dt.bfloat16
I32 = mybir.dt.int32
U8 = mybir.dt.uint8
OP = mybir.AluOpType
ACT = mybir.ActivationFunctionType
AX = mybir.AxisListType.X
BFnp = ml_dtypes.bfloat16

D = 1024
OPROJ = 3072
HID = 4096
KS = 128
TILE = 128
NHALO = 2
NT = 16
NALL = NHALO + NT
NW1T = HID // TILE
NW2T = D // TILE
NWELEM = float(HID * D)

_TAPS = ()
_LOCKSTEP = False

_TRIG_NAMES = ("wf1a", "wf1b", "wf2a", "wf2b", "winv1", "nwinv1", "winv2",
               "mc", "mi", "mb")


def _phase_pair(phase_int, sgn):
    P = np.mod(phase_int, 256).astype(np.int64)
    S = np.asarray(sgn, np.float64).copy()
    quarter = (P % 64 == 0)
    v0 = np.select([P == 0, P == 64, P == 128, P == 192], [0.0, 1.0, 0.0, -1.0])
    S = np.where(quarter, S * v0, S)
    P = np.where(quarter, 64, P)
    return P.astype(np.float32), S.astype(np.float32)


def _host_consts():
    N = 256
    t = np.arange(N)[:, None]
    f = np.arange(128)[None, :]
    c = {}

    def put(name, phase, sgn):
        P, S = _phase_pair(phase, np.broadcast_to(np.asarray(sgn, np.float64),
                                                  P0.shape))
        c["P_" + name] = P
        c["SGN_" + name] = S

    P0 = np.zeros((128, 128))
    put("wf1a", t[:128] * f + 64, 1.0)
    put("wf1b", t[128:] * f + 64, 1.0)
    ph = t * f
    ph[:, 0:1] = 128 * t + 64
    sg = np.full((N, 128), -1.0)
    sg[:, 0] = 1.0
    put("wf2a", ph[:128], sg[:128])
    put("wf2b", ph[128:], sg[128:])
    fr = np.arange(128)[:, None]
    j = np.arange(128, 256)[None, :]
    ph1 = fr * j + 64
    sg1 = np.where(fr == 0, 1.0 / N, 2.0 / N) * np.ones_like(j, dtype=np.float64)
    put("winv1", ph1, sg1)
    put("nwinv1", ph1, -sg1)
    ph2 = fr * j
    ph2[0:1, :] = 128 * j + 64
    sg2 = np.full((128, 128), -2.0 / N)
    sg2[0, :] = 1.0 / N
    put("winv2", ph2, sg2)
    kk = np.arange(KS)[:, None]
    put("mc", kk * f + 64, 1.0)
    phi = kk * f
    sgi = np.full((KS, 128), -1.0)
    sgi[:, 0] = 0.0
    put("mi", phi, sgi)
    phb = kk * f + 64
    phb[:, 0:1] = 128 * kk + 64
    put("mb", phb, 1.0)

    c["ident_bf"] = np.eye(128, dtype=BFnp)
    c["ident_f32"] = np.eye(128, dtype=np.float32)
    c["ones_row_bf"] = np.ones((1, 128), BFnp)
    c["ones_row_f32"] = np.ones((1, 128), np.float32)
    c["ones_col_f32"] = np.ones((128, 1), np.float32)
    c["ones_sq_f32"] = np.ones((128, 128), np.float32)
    return c

_CONST_SPECS = (
    [("P_" + n, [128, 128], F32, F32) for n in _TRIG_NAMES]
    + [("SGN_" + n, [128, 128], F32, F32) for n in _TRIG_NAMES]
    + [
        ("ident_bf", [128, 128], BF16, BF16), ("ident_f32", [128, 128], F32, F32),
        ("ones_row_bf", [1, 128], BF16, BF16), ("ones_row_f32", [1, 128], F32, F32),
        ("ones_col_f32", [128, 1], F32, F32), ("ones_sq_f32", [128, 128], F32, F32),
    ])
_TRIG_SBUF_DT = {"mc": F32, "mi": F32, "mb": F32}

_ENGS = ("tensor", "vector", "scalar", "gpsimd", "sync")
_DMA_LANES = {"DS": 8, "DQ": 4}
_SEMS = (("T", "V", "S", "Q")
         + tuple(f"DS{i}" for i in range(_DMA_LANES["DS"]))
         + tuple(f"DQ{i}" for i in range(_DMA_LANES["DQ"])))
_OWNER = {"T": "tensor", "V": "vector", "S": "scalar", "Q": "gpsimd"}


class Sch:
    def __init__(self):
        self.ops = {e: [] for e in _ENGS}
        self.cnt = {s: 0 for s in _SEMS}
        self.dma_rr = {"DS": 0, "DQ": 0}

    @staticmethod
    def _flat(deps):
        out = []
        def rec(d):
            if not d:
                return
            if isinstance(d, tuple) and len(d) == 2 and isinstance(d[0], str):
                out.append(d)
                return
            for x in d:
                rec(x)
        rec(tuple(deps))
        return tuple(out)

    def op(self, eng, fn, deps=(), inc=None):
        deps = self._flat(deps)
        if _LOCKSTEP:
            deps = self.barrier()
            if inc is None:
                inc = {"tensor": "T", "vector": "V", "scalar": "S",
                       "gpsimd": "Q", "sync": None}[eng]
        tok = None
        if inc:
            if inc in _DMA_LANES:
                lane = self.dma_rr[inc]
                self.dma_rr[inc] = (lane + 1) % _DMA_LANES[inc]
                sem = f"{inc}{lane}"
                if self.cnt[sem] > 0:
                    deps = deps + ((sem, self.cnt[sem]),)
                self.cnt[sem] += 16
                tok = (sem, self.cnt[sem])
                self.ops[eng].append((fn, deps, sem))
                return tok
            self.cnt[inc] += 1
            tok = (inc, self.cnt[inc])
        self.ops[eng].append((fn, deps, inc))
        return tok

    def barrier(self):
        return tuple((s, v) for s, v in self.cnt.items() if v > 0)

    def replay(self, eng_name, e, sems):
        waited = {}
        for fn, deps, inc in self.ops[eng_name]:
            for s, v in deps:
                if _OWNER.get(s) == eng_name:
                    continue
                if waited.get(s, 0) < v:
                    e.wait_ge(sems[s], v)
                    waited[s] = v
            inst = fn(e)
            if inc:
                assert inst is not None
                if inc.startswith("D"):
                    inst.then_inc(sems[inc], 16)
                else:
                    e.maybe_drain_then_inc((sems[inc], 1), fusable=True)


class Arena:

    def __init__(self, nc):
        self.nc = nc
        self.base = (nc.sbuf_base + 31) // 32 * 32
        self.total = nc.sbuf_top - self.base
        self.cur = 0
        self.marks = []

    def alloc(self, name, shape, dtype):
        per_part = int(np.prod(shape[1:])) * mybir.dt.size(dtype)
        off = (self.cur + 31) // 32 * 32
        self.cur = off + per_part
        assert self.cur <= self.total, \
            f"SBUF arena overflow at {name}: {self.cur} > {self.total}"
        return self.nc.alloc_sbuf_tensor_at(name, shape, dtype,
                                            offset=self.base + off)

    def push(self):
        self.marks.append(self.cur)

    def pop(self):
        self.cur = self.marks.pop()


def build_nc():
    nc = bass.Bass("TRN2", target_bir_lowering=False)

    dram = {}
    def din(name, shape, dtype=F32):
        dram[name] = nc.dram_tensor(name, shape, dtype, kind="ExternalInput").ap()
        return dram[name]

    x_own = din("x_own", [NT * TILE, D])
    x_halo = din("x_halo", [NHALO * TILE, D])
    hm = din("hm", [128, NHALO])
    W_in = din("W_in", [D, OPROJ])
    b_in = din("b_in", [1, OPROJ])
    k0_t = din("k0_t", [D, KS])
    k1_t = din("k1_t", [D, KS])
    gamma_t = din("gamma_t", [128, 8])
    beta_t = din("beta_t", [128, 8])
    W_out = din("W_out", [D, D])
    b_out = din("b_out", [1, D])
    nw1_r = din("nw1_r", [1, D])
    w1 = din("w1", [HID, D])
    as_i = din("as_i", [1, 1])
    ab_i = din("ab_i", [1, 1])
    nw2_r = din("nw2_r", [1, HID])
    w2 = din("w2", [D, HID])
    for cn, cs, _sd, cdd in _CONST_SPECS:
        din("c_" + cn, cs, cdd)

    out_d = nc.dram_tensor("out", [NT * TILE, D], F32, kind="ExternalOutput").ap()
    taps_d = {}
    for tname, _bk, tshape, tdt in _TAPS:
        taps_d[tname] = nc.dram_tensor(tname, tshape, tdt, kind="ExternalOutput").ap()

    sc_xp = nc.dram_tensor("sc_xp", [NT * TILE, D], BF16, kind="Internal").ap()
    sc_wq1 = nc.dram_tensor("sc_wq1", [HID, D], BF16, kind="Internal").ap()
    sc_wq2 = nc.dram_tensor("sc_wq2", [D, HID], BF16, kind="Internal").ap()

    ar = Arena(nc)
    A = ar.alloc

    cst = {}
    for cn, cs, csd, _dd in _CONST_SPECS:
        if cn.startswith("P_") or cn.startswith("SGN_"):
            continue
        cst[cn] = A("s_" + cn, cs, csd)
    for tn in _TRIG_NAMES:
        cst[tn] = A("s_" + tn, [128, 128], _TRIG_SBUF_DT.get(tn, BF16))
    hm_s = A("hm_s", [128, NHALO], F32)
    epsc = A("epsc", [128, 2], F32)
    asab = A("asab", [128, 2], F32)
    mc1b = A("mc1b", [128, 1], F32)
    mc2b = A("mc2b", [128, 1], F32)
    thr_row = A("thr_row", [1, 8], F32)
    wsum1 = A("wsum1", [128, NW1T], F32)
    wsum2 = A("wsum2", [128, NW2T], F32)
    wtot = A("wtot", [128, 2], F32)
    stat = {n: A("st_" + n, [128, 20], F32) for n in
            ("mu1", "m21", "rs1", "nmr1", "mu2", "m22", "rs2", "nmr2",
             "ssq", "amx1", "m1", "k1", "ss2", "amx2", "m2", "k2",
             "mu3", "m23", "rs3", "nmr3", "tmpa", "tmpb", "tmpc")}
    xp_t = [A(f"xp{i}", [128, D], BF16) for i in range(3)]
    out_t = A("outb", [128, D], F32)

    ar.push()
    Wg = A("Wg", [128, 8 * OPROJ], BF16)
    Wo = A("Wo", [128, 8 * D], BF16)
    Ktab = {k_: A("Ktab_" + k_, [128, D], BF16)
            for k_ in ("K0C", "K0I", "K0B", "K1C", "K1I", "K1B")}
    b_st = A("b_st", [1, OPROJ + D], BF16)
    b_eff = A("b_eff", [1, OPROJ], BF16)
    xr = [A(f"xr{i}", [128, D], F32) for i in range(4)]
    xn = [A(f"xn{i}", [128, D], BF16) for i in range(2)]
    xnT = [A(f"xnT{i}", [128, D], BF16) for i in range(2)]
    sq_sink = A("sq_sink", [128, D], F32)
    r_buf = A("r_buf", [128, D], F32)
    ar.push()
    wf32 = [A(f"wf32{i}", [128, OPROJ], F32) for i in range(2)]
    kT_b = A("kT_b", [128, D], F32)
    ar.pop()
    proj = [A(f"proj{i}", [128, OPROJ], BF16) for i in range(3)]
    y1 = [A(f"y1_{i}", [128, D], BF16) for i in range(3)]
    y2 = [A("y2_0", [128, D], BF16)]
    y2T = [A("y2T0", [128, D], BF16)]
    Pb = [A(f"Pb{i}", [128, 1024], BF16) for i in range(4)]
    a_peak = ar.cur
    ar.pop()

    ar.push()
    wqf = [A(f"wqf{i}", [128, HID], F32) for i in range(2)]
    wqb = [A(f"wqb{i}", [128, HID], BF16) for i in range(2)]
    wgt = A("wgt", [128, HID], BF16)
    totb = A("totb", [128, 1], F32)
    thrw = A("thrw", [128, 1], F32)
    nthrw = A("nthrw", [128, 1], F32)
    w_peak = ar.cur
    ar.pop()

    ar.push()
    wq1T = [A(f"wq1T{i}", [128, HID], BF16) for i in range(8)]
    wq2Tc = [A(f"wq2T{i}", [128, D], BF16) for i in range(32)]
    nw1r = A("nw1r", [128, D], BF16)
    nw2r = A("nw2r", [128, HID], BF16)
    t1 = A("t1", [128, D], F32)
    xq1 = A("xq1", [128, D], BF16)
    xq1T = A("xq1T", [128, D], BF16)
    r1s = A("r1s", [128, 1024], F32)
    t2f = A("t2f", [128, HID], F32)
    it2s = [A(f"it2s{i}", [128, 1024], I32) for i in range(2)]
    f2s = [A("f2s0", [128, 1024], BF16)]
    g2s = [A("g2s0", [128, 1024], BF16)]
    d2s = [A("d2s0", [128, 1024], BF16)]
    xq2o = [A(f"xq2o{i}", [128, 1024], BF16) for i in range(3)]
    xq2Tc = [A(f"xq2Tc{i}", [128, 1024], BF16) for i in range(2)]
    b_peak = ar.cur
    ar.pop()

    sch = Sch()

    psum_specs = [("ps_tr", [128, 1024], BF16), ("pj", [128, 1024], F32),
                  ("yc1", [128, 1024], F32), ("yc2", [128, 1024], F32),
                  ("ps_g", [128, 512], F32)]

    from contextlib import ExitStack
    ctx = ExitStack()
    ps = {n: ctx.enter_context(nc.psum_tensor(n, s, d)) for n, s, d in psum_specs}
    sems = {s: ctx.enter_context(nc.semaphore("sem_" + s)) for s in _SEMS}

    for _sn, _st in stat.items():
        sch.op("vector", (lambda st_: lambda e: e.memset(st_[:], 0.0))(_st), (), "V")
    tok_eps = sch.op("vector", lambda e: e.memset(epsc[:, 0:1], 1e-5), (), "V")
    tok_eps = sch.op("vector", lambda e: e.memset(epsc[:, 1:2], 1e-8), (tok_eps,), "V")

    free = {}
    tap_tok = {}

    def F(key):
        a, b = free.get(key), tap_tok.get(key)
        if b is None:
            return a
        return (a, b) if a is not None else b

    def pshalf(name, h):
        return ps[name][:, h * 512:(h + 1) * 512]

    def Fb(name):
        return (F(name + "0"), F(name + "1"))

    def setfree(name, tok):
        free[name + "0"] = tok
        free[name + "1"] = tok

    tok_cst = {}
    for cn, _s, csd, cdd in _CONST_SPECS:
        if cn.startswith("P_") or cn.startswith("SGN_"):
            continue
        if csd != cdd:
            tok_cst[cn] = sch.op("gpsimd", (lambda cn: lambda e: e.dma_start(
                out=cst[cn][:], in_=dram["c_" + cn]))(cn), (), "DQ")
        else:
            tok_cst[cn] = sch.op("sync", (lambda cn: lambda e: e.dma_start(
                out=cst[cn][:], in_=dram["c_" + cn]))(cn), (), "DS")

    _tp = 2.0 * np.pi
    _chi = float(np.float32(_tp / 256.0))
    _clo = float(np.float32(_tp / 256.0 - np.float64(_chi)))
    _thi = float(np.float32(_tp))
    _tlo = float(np.float32(_tp - np.float64(_thi)))
    _sin_set = None
    _main_set = None
    from concourse.hw_specs import get_activation_tables
    for _i, _funcs in enumerate(get_activation_tables(nc.m.arch).values()):
        fs = set(_funcs)
        if _sin_set is None and ACT.Sin in fs:
            _sin_set = _i
        if _main_set is None and {ACT.Identity, ACT.Square, ACT.Relu,
                                  ACT.Ln, ACT.Exp}.issubset(fs):
            _main_set = _i
    assert _sin_set is not None and _main_set is not None
    sch.op("scalar", (lambda sid: lambda e: e.add_instruction(
        mybir.InstLoadActFuncSet(
            name=nc.get_next_instruction_name(), act_func_set_id=sid,
            ins=[], outs=[])))(_sin_set), (), None)
    stage = wf32[0]
    sP, sG, sA, sB, sC = (stage[:, 0:128], stage[:, 128:256], stage[:, 256:384],
                          stage[:, 384:512], stage[:, 512:640])
    prev_gen = None
    for tn in _TRIG_NAMES:
        tokP = sch.op("sync", (lambda tn: lambda e: e.dma_start(
            out=sP, in_=dram["c_P_" + tn]))(tn), (prev_gen,), "DS")
        tokS = sch.op("sync", (lambda tn: lambda e: e.dma_start(
            out=sG, in_=dram["c_SGN_" + tn]))(tn), (prev_gen,), "DS")
        tg1 = sch.op("vector", (lambda: lambda e: e.tensor_scalar(
            sA, sP, _chi, None, OP.mult))(), (tokP,), "V")
        tg2 = sch.op("vector", (lambda: lambda e: e.scalar_tensor_tensor(
            sB, sP, _clo, sA, OP.mult, OP.add))(), (tg1,), "V")
        tg3 = sch.op("vector", (lambda: lambda e: e.tensor_scalar(
            sA, sP, 128.5, None, OP.is_gt))(), (tg2,), "V")
        tg4 = sch.op("vector", (lambda: lambda e: e.scalar_tensor_tensor(
            sC, sA, -_thi, sB, OP.mult, OP.add))(), (tg3,), "V")
        tg5 = sch.op("vector", (lambda: lambda e: e.scalar_tensor_tensor(
            sB, sA, -_tlo, sC, OP.mult, OP.add))(), (tg4,), "V")
        tg6 = sch.op("scalar", (lambda: lambda e: e.activation(
            sC, sB, ACT.Sin))(), (tg5,), "S")
        prev_gen = sch.op("vector", (lambda tn: lambda e: e.tensor_tensor(
            cst[tn][:], sC, sG, OP.mult))(tn), (tg6, tokS), "V")
        tok_cst[tn] = prev_gen
    sch.op("scalar", (lambda sid: lambda e: e.add_instruction(
        mybir.InstLoadActFuncSet(
            name=nc.get_next_instruction_name(), act_func_set_id=sid,
            ins=[], outs=[])))(_main_set), (prev_gen,), None)
    tok_hm = sch.op("sync", lambda e: e.dma_start(out=hm_s[:], in_=hm), (), "DS")
    tok_bin = sch.op("sync", lambda e: e.dma_start(
        out=thr_row[0:1, 0:1], in_=as_i), (), "DS")
    tok_abin = sch.op("sync", lambda e: e.dma_start(
        out=thr_row[0:1, 1:2], in_=ab_i), (), "DS")
    tok_ab_mm = sch.op("tensor", lambda e: e.matmul(
        ps["ps_g"][:, 0:2], cst["ones_row_f32"][:], thr_row[0:1, 0:2],
        start=True, stop=True), (tok_bin, tok_abin, tok_cst["ones_row_f32"]), "T")
    tok_asab = sch.op("vector", lambda e: e.tensor_copy(asab[:], ps["ps_g"][:, 0:2]),
                      (tok_ab_mm,), "V")
    free["ps_g"] = tok_asab

    tok_binrow = sch.op("gpsimd", lambda e: e.dma_start(
        out=b_st[0:1, 0:OPROJ], in_=b_in), (), "DQ")
    tok_boutrow = sch.op("gpsimd", lambda e: e.dma_start(
        out=b_st[0:1, OPROJ:OPROJ + D], in_=b_out), (), "DQ")

    tok_k = {}
    for ki, kd in ((0, k0_t), (1, k1_t)):
        tok_kload = sch.op("sync", (lambda kd: lambda e: e.dma_start(
            out=wf32[0][:, 0:8 * KS].rearrange("p (c k) -> p c k", c=8),
            in_=kd.rearrange("(c p) k -> p c k", p=128)))(kd),
            (F("wf32_0"),) if ki else (prev_gen,), "DS")
        tk_tr = None
        for c in range(8):
            tk_tr = sch.op("tensor", (lambda c: lambda e: e.transpose(
                ps["yc1"][:, c * 128:(c + 1) * 128], wf32[0][:, c * KS:(c + 1) * KS],
                cst["ident_f32"][:]))(c),
                (tok_kload, tok_cst["ident_f32"],
                 Fb("yc1") if c == 0 else None), "T")
        tok_kT = sch.op("vector", lambda e: e.tensor_copy(
            kT_b[:], ps["yc1"][:]), (tk_tr,), "V")
        setfree("yc1", tok_kT)
        free["wf32_0"] = tok_kT
        for tab, mat in (("C", "mc"), ("I", "mi"), ("B", "mb")):
            key = f"K{ki}{tab}"
            tmm = None
            for dc in range(2):
                tmm = sch.op("tensor", (lambda mat, dc: lambda e: e.matmul(
                    pshalf("pj", dc), cst[mat][:], kT_b[:, dc * 512:(dc + 1) * 512],
                    start=True, stop=True))(mat, dc),
                    (tok_kT, tok_cst[mat], F(f"pj{dc}")), "T" if dc == 1 else None)
            tev = sch.op("vector", (lambda key: lambda e: e.tensor_copy(
                Ktab[key][:], ps["pj"][:]))(key), (tmm,), "V")
            setfree("pj", tev)
        tok_k[ki] = tev

    bias_banks = [("pj", 0), ("pj", 1), ("yc1", 0), ("yc1", 1),
                  ("yc2", 0), ("yc2", 1)]
    gamma_s = stat["tmpa"]
    beta_s = stat["tmpb"]
    tok_gl = sch.op("sync", lambda e: e.dma_start(out=gamma_s[:, 0:8], in_=gamma_t), (), "DS")
    tok_bl = sch.op("sync", lambda e: e.dma_start(out=beta_s[:, 0:8], in_=beta_t), (), "DS")

    tok_wg_ev = {}
    tok_bias_mm = {}
    prev_wdma = [(), ()]
    for c in range(8):
        buf = c % 2
        tok_wdma = sch.op("sync", (lambda c, buf: lambda e: e.dma_start(
            out=wf32[buf][:], in_=W_in[c * 128:(c + 1) * 128, :]))(c, buf),
            prev_wdma[buf] + ((tok_k[1],) if c < 2 else ()), "DS")
        tok_wg_ev[c] = sch.op("vector", (lambda c, buf: lambda e: e.tensor_scalar(
            Wg[:, c * OPROJ:(c + 1) * OPROJ], wf32[buf][:],
            gamma_s[:, c:c + 1], None, OP.mult))(c, buf), (tok_wdma, tok_gl), "V")
        for oc in range(6):
            bn, bh = bias_banks[oc]
            tok_bias_mm[oc] = sch.op("tensor", (lambda c, buf, oc, bn, bh: lambda e:
                e.matmul(pshalf(bn, bh)[0:1, :], beta_s[:, c:c + 1],
                         wf32[buf][:, oc * 512:(oc + 1) * 512],
                         start=(c == 0), stop=(c == 7)))(c, buf, oc, bn, bh),
                (tok_wdma, tok_bl, F(f"{bn}{bh}") if c == 0 else None),
                "T" if oc == 5 else None)
        prev_wdma[buf] = (tok_wg_ev[c], tok_bias_mm[5])
    tok_beff = {}
    for oc in range(6):
        bn, bh = bias_banks[oc]
        tok_beff[oc] = sch.op("vector", (lambda oc, bn, bh: lambda e: e.tensor_tensor(
            b_eff[0:1, oc * 512:(oc + 1) * 512], pshalf(bn, bh)[0:1, :],
            b_st[0:1, oc * 512:(oc + 1) * 512], OP.add))(oc, bn, bh),
            (tok_bias_mm[5], tok_binrow), "V")
        free[f"{bn}{bh}"] = tok_beff[oc]
    tok_beff_all = tok_beff[5]

    tok_wo_all = []
    for c in range(8):
        tok_wo_all.append(sch.op("gpsimd", (lambda c: lambda e: e.dma_start(
            out=Wo[:, c * D:(c + 1) * D], in_=W_out[c * 128:(c + 1) * 128, :]))(c),
            (), "DQ"))
    tok_wo_all = tuple(tok_wo_all)

    conv_cst = tuple(tok_cst[n] for n in
                     ("wf1a", "wf1b", "wf2a", "wf2b", "winv1", "nwinv1", "winv2"))

    def emit_ln_stats(j, srcbuf, pre, eps, deps):
        mu, m2m, rs_, nmr = (stat["mu" + pre], stat["m2" + pre],
                             stat["rs" + pre], stat["nmr" + pre])
        tmu = sch.op("vector", (lambda: lambda e: e.reduce_sum(
            mu[:, j:j + 1], srcbuf[:], axis=AX))(), deps, "V")
        tm2 = sch.op("scalar", (lambda: lambda e: e.activation(
            sq_sink[:, 0:srcbuf.shape[-1]] if False else sq_sink[:],
            srcbuf[:], ACT.Square, accum_out=m2m[:, j:j + 1]))(), deps, "S")
        tmua = sch.op("scalar", (lambda: lambda e: e.activation(
            mu[:, j:j + 1], mu[:, j:j + 1], ACT.Identity, scale=1.0 / D))(), (tmu,), "S")
        tm2a = sch.op("scalar", (lambda: lambda e: e.activation(
            m2m[:, j:j + 1], m2m[:, j:j + 1], ACT.Identity, scale=1.0 / D,
            bias=epsc[:, 0:1]))(), (tm2, tok_eps), "S")
        tmu2 = sch.op("scalar", (lambda: lambda e: e.activation(
            stat["tmpc"][:, j:j + 1], mu[:, j:j + 1], ACT.Square))(), (tmua,), "S")
        tvar = sch.op("vector", (lambda: lambda e: e.tensor_tensor(
            m2m[:, j:j + 1], m2m[:, j:j + 1], stat["tmpc"][:, j:j + 1], OP.subtract))(),
            (tm2a, tmu2), "V")
        trr = sch.op("scalar", (lambda: lambda e: e.activation(
            rs_[:, j:j + 1], m2m[:, j:j + 1], ACT.Ln))(), (tvar,), "S")
        trs = sch.op("scalar", (lambda: lambda e: e.activation(
            rs_[:, j:j + 1], rs_[:, j:j + 1], ACT.Exp, scale=-0.5))(), (trr,), "S")
        tnmr = sch.op("vector", (lambda: lambda e: e.scalar_tensor_tensor(
            nmr[:, j:j + 1], mu[:, j:j + 1], -1.0, rs_[:, j:j + 1],
            OP.mult, OP.mult))(), (trs, tmua), "V")
        return trs, tnmr

    tok_x = [None] * NALL
    tok_xn = [None] * NALL
    tok_proj = [None] * NALL
    tok_y1 = [None] * NALL
    tok_y2 = [None] * NALL
    tok_r = [None] * NALL
    tok_xp_dma = [None] * NALL

    def tap(name, key_buf, deps, freekey=None):
        if not _TAPS:
            return
        for tname, bk, _s, _d in _TAPS:
            if bk == name:
                tk = sch.op("sync", (lambda tname, key_buf: lambda e: e.dma_start(
                    out=taps_d[tname], in_=key_buf[:]))(tname, key_buf), deps, "DS")
                if freekey:
                    p = tap_tok.get(freekey)
                    tap_tok[freekey] = tk if p is None else (p, tk)

    _PROJ_BANKS = ("pj", "yc1", "yc2")

    def emit_xload(j):
        xslot = j % 4
        src = x_halo[j * TILE:(j + 1) * TILE, :] if j < NHALO else \
            x_own[(j - NHALO) * TILE:(j - NHALO + 1) * TILE, :]
        tok_x[j] = sch.op("gpsimd", (lambda src, xslot: lambda e: e.dma_start(
            out=xr[xslot][:], in_=src))(src, xslot), (F(f"xr{xslot}"),), "DQ")

    def emit_ln1(j):
        xslot = j % 4
        trs, tnmr = emit_ln_stats(j, xr[xslot], "1", 1e-5, (tok_x[j],))
        xnb = j % 2
        tok_xn[j] = sch.op("scalar", (lambda j, xslot, xnb: lambda e: e.activation(
            xn[xnb][:], xr[xslot][:], ACT.Identity,
            bias=stat["nmr1"][:, j:j + 1], scale=stat["rs1"][:, j:j + 1]))(j, xslot, xnb),
            (trs, tnmr, F(f"xn{xnb}")), "S")
        if j < NHALO:
            free[f"xr{xslot}"] = tok_xn[j]

    def emit_front(j):
        xnb = j % 2
        ttr = None
        for c in range(8):
            ttr = sch.op("tensor", (lambda c, xnb: lambda e: e.transpose(
                ps["ps_tr"][:, c * 128:(c + 1) * 128],
                xn[xnb][:, c * 128:(c + 1) * 128], cst["ident_bf"][:]))(c, xnb),
                (tok_xn[j], F("ps_tr") if c == 0 else None,
                 tok_cst["ident_bf"] if j == 0 and c == 0 else None),
                "T" if c == 7 else None)
        tevt = sch.op("vector", (lambda xnb: lambda e: e.tensor_copy(
            xnT[xnb][:], ps["ps_tr"][:]))(xnb),
            (ttr, F(f"xnT{xnb}")), "V")
        free["ps_tr"] = tevt
        free[f"xn{xnb}"] = ttr
        trn = [tevt] * 8
        pslot = j % 3
        ev_last = None
        tmm = None
        for p in range(3):
            bankT = _PROJ_BANKS[p]
            for h in range(2):
                oc = 2 * p + h
                for c in range(8):
                    tmm = sch.op("tensor", (lambda oc, c, h, xnb, bankT: lambda e:
                        e.matmul(pshalf(bankT, h), xnT[xnb][:, c * 128:(c + 1) * 128],
                                 Wg[:, c * OPROJ + oc * 512: c * OPROJ + oc * 512 + 512],
                                 start=(c == 0), stop=False))(oc, c, h, xnb, bankT),
                        (trn[c], F(f"{bankT}{h}") if c == 0 else None,
                         tok_wg_ev[c] if j == 0 else None),
                        None)
                tmm = sch.op("tensor", (lambda oc, h, bankT: lambda e: e.matmul(
                    pshalf(bankT, h), cst["ones_row_bf"][:],
                    b_eff[0:1, oc * 512:oc * 512 + 512],
                    start=False, stop=True))(oc, h, bankT),
                    (tok_beff_all if j == 0 and p == 0 else None,),
                    "T" if h == 1 else None)
            if j < NHALO and p == 0:
                tev = sch.op("vector", (lambda j, p, pslot, bankT: lambda e:
                    e.tensor_scalar(proj[pslot][:, p * 1024:(p + 1) * 1024],
                                    ps[bankT][:], hm_s[:, j:j + 1], None,
                                    OP.mult))(j, p, pslot, bankT),
                    (tmm, tok_hm, F(f"proj{pslot}")), "V")
            else:
                tev = sch.op("vector", (lambda p, pslot, bankT: lambda e:
                    e.tensor_copy(proj[pslot][:, p * 1024:(p + 1) * 1024],
                                  ps[bankT][:]))(p, pslot, bankT),
                    (tmm, F(f"proj{pslot}")), "V")
            setfree(bankT, tev)
            ev_last = tev
        tok_proj[j] = ev_last
        free[f"xnT{xnb}"] = tmm

    def conv_hop(jj, src_bufs, src_toks, gate_buf, gate_off, gate_tok,
                 Kpre, dst_buf, dst_free_key, extra=()):
        tf1 = tf2 = None
        for dc in range(2):
            tf1 = sch.op("tensor", (lambda dc: lambda e: e.matmul(
                pshalf("yc1", dc), cst["wf1a"][:],
                src_bufs[0][:, dc * 512:(dc + 1) * 512],
                start=True, stop=False))(dc),
                (src_toks[0], F(f"yc1{dc}")) + (extra if dc == 0 else ()), None)
            tf1 = sch.op("tensor", (lambda dc: lambda e: e.matmul(
                pshalf("yc1", dc), cst["wf1b"][:],
                src_bufs[1][:, dc * 512:(dc + 1) * 512],
                start=False, stop=True))(dc), (src_toks[1],),
                "T" if dc == 1 else None)
        for dc in range(2):
            tf2 = sch.op("tensor", (lambda dc: lambda e: e.matmul(
                pshalf("yc2", dc), cst["wf2a"][:],
                src_bufs[0][:, dc * 512:(dc + 1) * 512],
                start=True, stop=False))(dc), (F(f"yc2{dc}"),), None)
            tf2 = sch.op("tensor", (lambda dc: lambda e: e.matmul(
                pshalf("yc2", dc), cst["wf2b"][:],
                src_bufs[1][:, dc * 512:(dc + 1) * 512],
                start=False, stop=True))(dc), (),
                "T" if dc == 1 else None)
        tp = []
        for pi, (bank, tabk) in enumerate(
                (("yc1", Kpre + "C"), ("yc1", Kpre + "I"),
                 ("yc2", Kpre + "I"), ("yc2", Kpre + "B"))):
            tdep = tf1 if bank == "yc1" else tf2
            tp.append(sch.op("vector", (lambda pi, bank, tabk: lambda e:
                e.tensor_tensor(Pb[pi][:], ps[bank][:], Ktab[tabk][:],
                                OP.mult))(pi, bank, tabk),
                (tdep, F(f"Pb{pi}")), "V"))
        setfree("yc1", tp[1])
        setfree("yc2", tp[3])
        tiv = None
        for dc in range(2):
            for mi_, (wkey, pi) in enumerate(
                    (("winv1", 0), ("nwinv1", 2), ("winv2", 1), ("winv2", 3))):
                tiv = sch.op("tensor", (lambda wkey, pi, dc, mi_: lambda e: e.matmul(
                    pshalf("pj", dc), cst[wkey][:],
                    Pb[pi][:, dc * 512:(dc + 1) * 512],
                    start=(mi_ == 0), stop=(mi_ == 3)))(wkey, pi, dc, mi_),
                    (tp[pi], F(f"pj{dc}")) if mi_ == 0 else (tp[pi],),
                    "T" if (dc == 1 and mi_ == 3) else None)
        for pi in range(4):
            free[f"Pb{pi}"] = tiv
        tg = sch.op("vector", (lambda gate_off: lambda e: e.tensor_tensor(
            dst_buf[:], ps["pj"][:],
            gate_buf[:, gate_off:gate_off + 1024], OP.mult))(gate_off),
            (tiv, gate_tok, F(dst_free_key)), "V")
        setfree("pj", tg)
        return tg

    def emit_conv1(j):
        pprev, pcur = (j - 1) % 3, j % 3
        y1slot = j % 3
        tok_y1[j] = conv_hop(
            j, (proj[pprev], proj[pcur]), (tok_proj[j - 1], tok_proj[j]),
            proj[pcur], D, tok_proj[j], "K0", y1[y1slot], f"y1_{y1slot}",
            extra=conv_cst if j == 1 else ())

    def emit_conv2_wout(j):
        pcur = j % 3
        xslot = j % 4
        tok_y2[j] = conv_hop(
            j, (y1[(j - 1) % 3], y1[j % 3]), (tok_y1[j - 1], tok_y1[j]),
            proj[pcur], 2 * D, tok_proj[j], "K1", y2[0], "y2_0")
        free[f"y1_{(j - 1) % 3}"] = tok_y2[j]
        free[f"proj{pcur}"] = (free.get(f"proj{pcur}"), tok_y2[j])
        ttr2 = None
        for c in range(8):
            ttr2 = sch.op("tensor", (lambda c: lambda e: e.transpose(
                ps["ps_tr"][:, c * 128:(c + 1) * 128],
                y2[0][:, c * 128:(c + 1) * 128], cst["ident_bf"][:]))(c),
                (tok_y2[j], F("ps_tr") if c == 0 else None),
                "T" if c == 7 else None)
        tevt2 = sch.op("vector", (lambda: lambda e: e.tensor_copy(
            y2T[0][:], ps["ps_tr"][:]))(), (ttr2, F("y2T0")), "V")
        free["ps_tr"] = tevt2
        free["y2_0"] = ttr2
        trn2 = [tevt2] * 8
        tmm2 = None
        for oc in range(2):
            for c in range(8):
                tmm2 = sch.op("tensor", (lambda oc, c: lambda e: e.matmul(
                    pshalf("pj", oc), y2T[0][:, c * 128:(c + 1) * 128],
                    Wo[:, c * D + oc * 512: c * D + oc * 512 + 512],
                    start=(c == 0), stop=False))(oc, c),
                    (trn2[c], F(f"pj{oc}") if c == 0 else None)
                    + (tok_wo_all if j == 2 and oc == 0 and c == 0 else ()),
                    None)
            tmm2 = sch.op("tensor", (lambda oc: lambda e: e.matmul(
                pshalf("pj", oc), cst["ones_row_bf"][:],
                b_st[0:1, OPROJ + oc * 512:OPROJ + oc * 512 + 512],
                start=False, stop=True))(oc),
                (tok_boutrow if j == 2 and oc == 0 else None,),
                "T" if oc == 1 else None)
        ev_r = sch.op("vector", (lambda xslot: lambda e:
            e.scalar_tensor_tensor(r_buf[:], xr[xslot][:], 2.0,
                                   ps["pj"][:], OP.mult, OP.add))(xslot),
            (tmm2, tok_x[j], F("r_buf")), "V")
        setfree("pj", ev_r)
        tok_r[j] = ev_r
        free["y2T0"] = tmm2
        free[f"xr{xslot}"] = ev_r
        trs2, tnmr2 = emit_ln_stats(j, r_buf, "2", 1e-5, (tok_r[j],))
        xpb = j % 2
        tok_xp = sch.op("scalar", (lambda j, xpb: lambda e: e.activation(
            xp_t[xpb][:], r_buf[:], ACT.Identity,
            bias=stat["nmr2"][:, j:j + 1], scale=stat["rs2"][:, j:j + 1]))(j, xpb),
            (trs2, tnmr2, F(f"xp{xpb}")), "S")
        free["r_buf"] = tok_xp
        tok_xp_dma[j] = sch.op("gpsimd", (lambda j, xpb: lambda e: e.dma_start(
            out=sc_xp[(j - NHALO) * TILE:(j - NHALO + 1) * TILE, :],
            in_=xp_t[xpb][:]))(j, xpb), (tok_xp,), "DQ")
        free[f"xp{xpb}"] = tok_xp_dma[j]

    emit_xload(0)
    emit_ln1(0)
    emit_xload(1)
    for i in range(NALL + 2):
        if i < NALL:
            emit_front(i)
        if 1 <= i - 1 < NALL:
            emit_conv1(i - 1)
        if i + 1 < NALL:
            emit_ln1(i + 1)
        if 2 <= i - 2 < NALL:
            emit_conv2_wout(i - 2)
        if i + 2 < NALL:
            emit_xload(i + 2)

    barrierA = sch.barrier()

    def wquant(wdram, ntiles, width, wsum, scdram, mc_t, wtcol):
        toks_sum = []
        prev = [F("wqf0"), F("wqf1")]
        for i in range(ntiles):
            buf = i % 2
            td = sch.op("gpsimd", (lambda i, buf: lambda e: e.dma_start(
                out=wqf[buf][:, 0:width], in_=wdram[i * 128:(i + 1) * 128, :]))(i, buf),
                barrierA + ((prev[buf],) if prev[buf] else ()), "DQ")
            ts_ = sch.op("vector", (lambda i, buf: lambda e: e.reduce_sum(
                wsum[:, i:i + 1], wqf[buf][:, 0:width], axis=AX,
                apply_absolute_value=True))(i, buf), (td,), "V")
            prev[buf] = ts_
            toks_sum.append(ts_)
        trow = sch.op("vector", (lambda: lambda e: e.reduce_sum(
            wtot[:, wtcol:wtcol + 1], wsum[:, 0:ntiles], axis=AX))(),
            tuple(toks_sum), "V")
        tap(f"wsum_{wtcol}", wsum, (trow,))
        tap(f"wtot_{wtcol}", wtot, (trow,), freekey="wtot")
        tmm = sch.op("tensor", (lambda: lambda e: e.matmul(
            ps["ps_g"][:, 0:1], cst["ones_sq_f32"][:], wtot[:, wtcol:wtcol + 1],
            start=True, stop=True))(), (trow, F("ps_g"), tok_cst["ones_sq_f32"]), "T")
        tev = sch.op("vector", (lambda: lambda e: e.tensor_copy(
            totb[:], ps["ps_g"][:, 0:1]))(), (tmm, F("totb")), "V")
        free["ps_g"] = tev
        tap(f"totb_{wtcol}", totb, (tev,), freekey="totb")
        tmc = sch.op("vector", (lambda: lambda e: e.tensor_scalar(
            mc_t[:], totb[:], 1.0 / NWELEM, 1e-5, OP.mult, OP.max))(), (tev,), "V")
        free["totb"] = tmc
        tth = sch.op("vector", (lambda: lambda e: e.tensor_scalar(
            thrw[:], mc_t[:], 0.5, None, OP.mult))(), (tmc, F("thrw")), "V")
        tnt = sch.op("vector", (lambda: lambda e: e.tensor_scalar(
            nthrw[:], thrw[:], -1.0, None, OP.mult))(), (tth, F("nthrw")), "V")
        prev_d = [None, None]
        tlast = None
        tstores = []
        for i in range(ntiles):
            buf = i % 2
            td = sch.op("gpsimd", (lambda i, buf: lambda e: e.dma_start(
                out=wqf[buf][:, 0:width], in_=wdram[i * 128:(i + 1) * 128, :]))(i, buf),
                ((prev_d[buf],) if prev_d[buf] else (toks_sum[-1],)), "DQ")
            tg_ = sch.op("vector", (lambda i, buf: lambda e: e.tensor_scalar(
                wgt[:, 0:width], wqf[buf][:, 0:width], nthrw[:], None,
                OP.is_lt))(i, buf), (td, tnt, F("wgt")), "V")
            tq_ = sch.op("vector", (lambda i, buf: lambda e: e.scalar_tensor_tensor(
                wqb[buf][:, 0:width], wqf[buf][:, 0:width], thrw[:],
                wgt[:, 0:width], OP.is_gt, OP.subtract))(i, buf),
                (tg_, F(f"wqb{buf}")), "V")
            free["wgt"] = tq_
            prev_d[buf] = tq_
            tst = sch.op("sync", (lambda i, buf: lambda e: e.dma_start(
                out=scdram[i * 128:(i + 1) * 128, :], in_=wqb[buf][:, 0:width]))(i, buf),
                (tq_,), "DS")
            free[f"wqb{buf}"] = tst
            tlast = tst
            tstores.append(tst)
        free["thrw"] = tlast
        free["nthrw"] = tlast
        free["wqf0"] = prev_d[0]
        free["wqf1"] = prev_d[1]
        return tuple(tstores)

    tok_wq1 = wquant(w1, NW1T, D, wsum1, sc_wq1, mc1b, 0)
    tok_wq2 = wquant(w2, NW2T, HID, wsum2, sc_wq2, mc2b, 1)
    tap("mc1b", mc1b, tok_wq1)
    tap("mc2b", mc2b, tok_wq2)

    barrierW = sch.barrier()
    tok_wq1T = [None] * 8
    for dc in range(8):
        tok_wq1T[dc] = sch.op("sync", (lambda dc: lambda e: e.dma_start(
            out=wq1T[dc][:], in_=sc_wq1[:, dc * 128:(dc + 1) * 128],
            transpose=True))(dc), barrierW + tok_wq1, "DS")
    for dc in range(8):
        tap(f"wq1T{dc}", wq1T[dc], (tok_wq1T[dc],))
    tok_wq2T = [None] * 32
    for hc in range(32):
        tok_wq2T[hc] = sch.op("sync", (lambda hc: lambda e: e.dma_start(
            out=wq2Tc[hc][:], in_=sc_wq2[:, hc * 128:(hc + 1) * 128],
            transpose=True))(hc), barrierW + tok_wq2, "DS")

    tok_nw1row = sch.op("sync", lambda e: e.dma_start(
        out=t1[0:1, 0:D], in_=nw1_r), barrierW, "DS")
    tok_nw2row = sch.op("sync", lambda e: e.dma_start(
        out=t2f[0:1, 0:HID], in_=nw2_r), barrierW, "DS")
    tok_nwb = {}
    for oc in range(2):
        tm = sch.op("tensor", (lambda oc: lambda e: e.matmul(
            ps["ps_g"][:], cst["ones_row_f32"][:], t1[0:1, oc * 512:(oc + 1) * 512],
            start=True, stop=True))(oc), barrierW + (tok_nw1row, F("ps_g")), "T")
        tok_nwb[oc] = sch.op("vector", (lambda oc: lambda e: e.tensor_copy(
            nw1r[:, oc * 512:(oc + 1) * 512], ps["ps_g"][:]))(oc), (tm,), "V")
        free["ps_g"] = tok_nwb[oc]
    tok_nw1r = tok_nwb[1]
    tok_nw1r_T = tm
    for oc in range(8):
        tm = sch.op("tensor", (lambda oc: lambda e: e.matmul(
            ps["ps_g"][:], cst["ones_row_f32"][:], t2f[0:1, oc * 512:(oc + 1) * 512],
            start=True, stop=True))(oc), (tok_nw2row, F("ps_g")), "T")
        tok_nwb[oc] = sch.op("vector", (lambda oc: lambda e: e.tensor_copy(
            nw2r[:, oc * 512:(oc + 1) * 512], ps["ps_g"][:]))(oc), (tm,), "V")
        free["ps_g"] = tok_nwb[oc]
    tok_nw2r = tok_nwb[7]
    free["t1"] = tok_nw1r_T
    for _k in range(4):
        free[f"t2f{_k}"] = tm

    def quant_round(dst_bf, src, m_col, itbuf, fbuf, gbuf, dbuf, width, deps,
                    levels=1):
        tcast = sch.op("scalar", (lambda: lambda e: e.activation(
            itbuf[:, 0:width], src[:, 0:width], ACT.Identity, scale=m_col))(),
            deps, "S")
        tf = sch.op("vector", (lambda: lambda e: e.scalar_tensor_tensor(
            fbuf[:, 0:width], src[:, 0:width], m_col, itbuf[:, 0:width],
            OP.mult, OP.subtract))(), (tcast,), "V")
        tabs = sch.op("scalar", (lambda: lambda e: e.activation(
            dbuf[:, 0:width], fbuf[:, 0:width], ACT.Abs))(), (tf,), "S")
        tsgn = sch.op("scalar", (lambda: lambda e: e.activation(
            fbuf[:, 0:width], fbuf[:, 0:width], ACT.Sign))(), (tabs,), "S")
        tn = sch.op("vector", (lambda: lambda e: e.tensor_scalar(
            gbuf[:, 0:width], dbuf[:, 0:width], 0.5, None, OP.is_gt))(), (tabs,), "V")
        for lv in range(1, levels):
            tn = sch.op("vector", (lambda lv: lambda e: e.scalar_tensor_tensor(
                gbuf[:, 0:width], dbuf[:, 0:width], 0.5 + lv, gbuf[:, 0:width],
                OP.is_gt, OP.add))(lv), (tn,), "V")
        td_ = sch.op("vector", (lambda: lambda e: e.tensor_tensor(
            dbuf[:, 0:width], fbuf[:, 0:width], gbuf[:, 0:width], OP.mult))(),
            (tn, tsgn), "V")
        tq_ = sch.op("vector", (lambda: lambda e: e.tensor_tensor(
            dst_bf[:, 0:width], itbuf[:, 0:width], dbuf[:, 0:width], OP.add))(),
            (td_,), "V")
        return tq_, tf

    def rms_quant_chain(t, pre, ss_src_tok, amax_tok, mcb, nelem):
        ssn, amx, mcol, kcol = (stat["ss" + pre] if ("ss" + pre) in stat else stat["ssq"],
                                stat["amx" + pre], stat["m" + pre], stat["k" + pre])
        ta = sch.op("scalar", (lambda: lambda e: e.activation(
            stat["tmpa"][:, t:t + 1], ssn[:, t:t + 1], ACT.Identity,
            scale=1.0 / nelem, bias=epsc[:, 1:2]))(), (ss_src_tok, tok_eps), "S")
        tb = sch.op("scalar", (lambda: lambda e: e.activation(
            stat["tmpb"][:, t:t + 1], stat["tmpa"][:, t:t + 1], ACT.Ln))(), (ta,), "S")
        trs = sch.op("scalar", (lambda: lambda e: e.activation(
            stat["tmpa"][:, t:t + 1], stat["tmpb"][:, t:t + 1], ACT.Exp,
            scale=-0.5))(), (tb,), "S")
        tc = sch.op("vector", (lambda: lambda e: e.tensor_tensor(
            stat["tmpc"][:, t:t + 1], amx[:, t:t + 1], stat["tmpa"][:, t:t + 1],
            OP.mult))(), (trs, amax_tok), "V")
        tc2 = sch.op("vector", (lambda: lambda e: e.tensor_scalar(
            stat["tmpc"][:, t:t + 1], stat["tmpc"][:, t:t + 1], 1e-5, None, OP.max))(),
            (tc,), "V")
        trc0 = sch.op("scalar", (lambda: lambda e: e.activation(
            stat["tmpb"][:, t:t + 1], stat["tmpc"][:, t:t + 1], ACT.Ln))(), (tc2,), "S")
        trc = sch.op("scalar", (lambda: lambda e: e.activation(
            stat["tmpb"][:, t:t + 1], stat["tmpb"][:, t:t + 1], ACT.Exp,
            scale=-1.0))(), (trc0,), "S")
        tm_ = sch.op("vector", (lambda: lambda e: e.scalar_tensor_tensor(
            mcol[:, t:t + 1], stat["tmpb"][:, t:t + 1], 127.0, stat["tmpa"][:, t:t + 1],
            OP.mult, OP.mult))(), (trc, F("stm" + pre)), "V")
        tk_ = sch.op("vector", (lambda: lambda e: e.scalar_tensor_tensor(
            kcol[:, t:t + 1], stat["tmpc"][:, t:t + 1], 1.0 / 127.0, mcb,
            OP.mult, OP.mult))(), (tc2, F("stk" + pre)), "V")
        return tm_, tk_

    tok_out_dma = []
    itp = [0]
    Fst = {}
    Sst = {}

    def emit_stageF(t):
        xpb = t % 3
        d = {"xpb": xpb}
        tok_xpl = sch.op("sync", (lambda t, xpb: lambda e: e.dma_start(
            out=xp_t[xpb][:], in_=sc_xp[t * TILE:(t + 1) * TILE, :]))(t, xpb),
            (F(f"xp{xpb}"), tok_xp_dma[t + NHALO]), "DS")
        tss = sch.op("scalar", (lambda t, xpb: lambda e: e.activation(
            t1[:], xp_t[xpb][:], ACT.Square,
            accum_out=stat["ssq"][:, t:t + 1]))(t, xpb),
            (tok_xpl, F("t1")), "S")
        tt1 = sch.op("vector", (lambda xpb: lambda e: e.tensor_tensor(
            t1[:], xp_t[xpb][:], nw1r[:], OP.mult))(xpb),
            (tss, tok_xpl, tok_nw1r), "V")
        tam = sch.op("vector", (lambda t: lambda e: e.reduce_max(
            stat["amx1"][:, t:t + 1], t1[:], axis=AX,
            apply_absolute_value=True))(t), (tt1,), "V")
        tm1, tk1 = rms_quant_chain(t, "1", tss, tam, mc1b[:], D)
        d["tk1"] = tk1
        tq1, _tf1 = quant_round(xq1, t1, stat["m1"][:, t:t + 1],
                                it2s[itp[0]], xq1, g2s[0], d2s[0], 1024,
                                (tm1, tt1, F("xq1"), F(f"it{itp[0]}")),
                                levels=3)
        free[f"it{itp[0]}"] = tq1
        itp[0] ^= 1
        free["t1"] = tq1
        trq1 = []
        for c in range(8):
            trq1.append(sch.op("sync", (lambda c: lambda e: e.dma_start(
                out=xq1T[:, c * 128:(c + 1) * 128],
                in_=xq1[:, c * 128:(c + 1) * 128], transpose=True))(c),
                (tq1, F("xq1T") if c == 0 else None), "DS"))
        d["trq1"] = trq1
        free["xq1"] = tuple(trq1)
        return d

    def emit_chunkSV(tp_, k):
        tq2, tf2 = quant_round(
            xq2o[k % 3], t2f[:, k * 1024:(k + 1) * 1024],
            stat["m2"][:, tp_:tp_ + 1], it2s[itp[0]], f2s[0], g2s[0],
            d2s[0], 1024,
            (Sst[tp_]["tok_t2"][k], Sst[tp_]["tm2"], F(f"it{itp[0]}"),
             F(f"xq2o{k % 3}")), levels=5)
        free[f"it{itp[0]}"] = tq2
        itp[0] ^= 1
        free[f"t2f{k}"] = tf2
        return tq2

    def emit_superstrip(t, ss):
        tmm = [None, None]
        for h in range(2):
            for dc in range(8):
                tmm[h] = sch.op("tensor",
                    (lambda ss, dc, h: lambda e: e.matmul(
                        pshalf("pj", h), xq1T[:, dc * 128:(dc + 1) * 128],
                        wq1T[dc][:, ss * 1024 + h * 512:
                                   ss * 1024 + (h + 1) * 512],
                        start=(dc == 0), stop=(dc == 7)))(ss, dc, h),
                    (Fst[t]["trq1"][dc],
                     F(f"pj{h}") if dc == 0 else None,
                     tok_wq1T[dc] if t == 0 else None),
                    "T" if dc == 7 else None)
        trel = sch.op("scalar", (lambda t: lambda e: e.activation(
            r1s[:], ps["pj"][:], ACT.Relu,
            scale=stat["k1"][:, t:t + 1]))(t),
            (tmm[0], tmm[1], Fst[t]["tk1"], F("r1s")), "S")
        setfree("pj", trel)
        tsq = sch.op("scalar", (lambda: lambda e: e.activation(
            r1s[:], r1s[:], ACT.Square))(), (trel,), "S")
        th2 = sch.op("scalar", (lambda ss: lambda e: e.activation(
            t2f[:, ss * 1024:(ss + 1) * 1024], r1s[:], ACT.Identity,
            scale=asab[:, 0:1], bias=asab[:, 1:2]))(ss),
            (tsq, tok_asab if t == 0 else None, F(f"t2f{ss}")), "S")
        free["r1s"] = th2
        if ss == 3:
            free["xq1T"] = tmm[1]
        return th2

    def emit_chunkT(tp_, k, tq2):
        kb = k % 2
        ko = k % 3
        trk = []
        for c in range(8):
            trk.append(sch.op("sync", (lambda c, kb, ko: lambda e: e.dma_start(
                out=xq2Tc[kb][:, c * 128:(c + 1) * 128],
                in_=xq2o[ko][:, c * 128:(c + 1) * 128],
                transpose=True))(c, kb, ko),
                (tq2, F(f"xq2Tc{kb}") if c == 0 else None), "DS"))
        tmm2 = None
        for oc in range(2):
            for q in range(8):
                tmm2 = sch.op("tensor",
                    (lambda k, q, oc, kb: lambda e: e.matmul(
                        pshalf("yc1", oc),
                        xq2Tc[kb][:, q * 128:(q + 1) * 128],
                        wq2Tc[k * 8 + q][:, oc * 512:(oc + 1) * 512],
                        start=(k == 0 and q == 0),
                        stop=(k == 3 and q == 7),
                        skip_group_check=True))(k, q, oc, kb),
                    (trk[q],
                     tok_wq2T[k * 8 + q] if tp_ == 0 and oc == 0 else None,
                     F(f"yc1{oc}") if (k == 0 and q == 0) else None),
                    "T" if (oc == 1 and q == 7) else None)
        free[f"xq2Tc{kb}"] = tmm2
        free[f"xq2o{ko}"] = tuple(trk)
        return tmm2

    def emit_stat_chunk(t, k, th2_tok):
        tss2 = sch.op("scalar", (lambda k: lambda e: e.activation(
            t1[:], t2f[:, k * 1024:(k + 1) * 1024], ACT.Square,
            accum_out=stat["ss2"][:, 16 + k:17 + k]))(k),
            (th2_tok, F("t1")), "S")
        tt2 = sch.op("vector", (lambda k: lambda e: e.tensor_tensor(
            t2f[:, k * 1024:(k + 1) * 1024],
            t2f[:, k * 1024:(k + 1) * 1024],
            nw2r[:, k * 1024:(k + 1) * 1024], OP.mult))(k),
            (tss2, tok_nw2r if t == 0 else None), "V")
        tam2 = sch.op("vector", (lambda k: lambda e: e.reduce_max(
            stat["amx2"][:, 16 + k:17 + k],
            t2f[:, k * 1024:(k + 1) * 1024], axis=AX,
            apply_absolute_value=True))(k), (tt2,), "V")
        return tss2, tt2, tam2

    def emit_stats(t, tok_sc):
        d = {}
        tok_ss2 = [x[0] for x in tok_sc]
        tok_t2 = [x[1] for x in tok_sc]
        tok_am2 = [x[2] for x in tok_sc]
        tssc = sch.op("vector", (lambda t: lambda e: e.reduce_sum(
            stat["ss2"][:, t:t + 1], stat["ss2"][:, 16:20], axis=AX))(t),
            tuple(tok_ss2), "V")
        tamc = sch.op("vector", (lambda t: lambda e: e.reduce_max(
            stat["amx2"][:, t:t + 1], stat["amx2"][:, 16:20], axis=AX))(t),
            tuple(tok_am2), "V")
        tm2, tk2 = rms_quant_chain(t, "2", tssc, tamc, mc2b[:], HID)
        d["tm2"], d["tk2"] = tm2, tk2
        d["tok_t2"] = tok_t2
        return d

    emit_taps = None

    Fst[0] = None
    for t in range(NT + 1):
        if t == 0:
            Fst[0] = emit_stageF(0)
        tok_sc = [None] * 4
        tq2s = {}
        tmm2_last = None
        for k in range(6):
            if t >= 1 and k < 4:
                tq2s[k] = emit_chunkSV(t - 1, k)
            if t < NT and 1 <= k <= 4:
                th2_tok = emit_superstrip(t, k - 1)
                tok_sc[k - 1] = emit_stat_chunk(t, k - 1, th2_tok)
            if t >= 1 and k >= 2:
                tmm2_last = emit_chunkT(t - 1, k - 2, tq2s[k - 2])
        if t < NT:
            Sst[t] = emit_stats(t, tok_sc)
        if t + 1 < NT:
            Fst[t + 1] = emit_stageF(t + 1)
        if t >= 1:
            tp_ = t - 1
            pxpb = tp_ % 3
            ev3 = sch.op("vector", (lambda tp_, pxpb: lambda e:
                e.scalar_tensor_tensor(out_t[:], ps["yc1"][:],
                                       stat["k2"][:, tp_:tp_ + 1],
                                       xp_t[pxpb][:], OP.mult, OP.add))(tp_, pxpb),
                (tmm2_last, Sst[tp_]["tk2"], F("outb")), "V")
            setfree("yc1", ev3)
            free[f"xp{pxpb}"] = ev3
        if t >= 1:
            tp_ = t - 1
            tmu3 = sch.op("vector", (lambda tp_: lambda e: e.reduce_sum(
                stat["mu3"][:, tp_:tp_ + 1], out_t[:], axis=AX))(tp_), (ev3,), "V")
            tm23 = sch.op("scalar", (lambda tp_: lambda e: e.activation(
                t1[:], out_t[:], ACT.Square,
                accum_out=stat["m23"][:, tp_:tp_ + 1]))(tp_),
                (ev3, F("t1")), "S")
            tmua = sch.op("scalar", (lambda tp_: lambda e: e.activation(
                stat["mu3"][:, tp_:tp_ + 1], stat["mu3"][:, tp_:tp_ + 1],
                ACT.Identity, scale=1.0 / D))(tp_), (tmu3,), "S")
            tm2a = sch.op("scalar", (lambda tp_: lambda e: e.activation(
                stat["m23"][:, tp_:tp_ + 1], stat["m23"][:, tp_:tp_ + 1],
                ACT.Identity, scale=1.0 / D, bias=epsc[:, 0:1]))(tp_),
                (tm23, tok_eps), "S")
            tmu2 = sch.op("scalar", (lambda tp_: lambda e: e.activation(
                stat["tmpc"][:, tp_:tp_ + 1], stat["mu3"][:, tp_:tp_ + 1],
                ACT.Square))(tp_), (tmua,), "S")
            tvar = sch.op("vector", (lambda tp_: lambda e: e.tensor_tensor(
                stat["m23"][:, tp_:tp_ + 1], stat["m23"][:, tp_:tp_ + 1],
                stat["tmpc"][:, tp_:tp_ + 1], OP.subtract))(tp_),
                (tm2a, tmu2), "V")
            trr = sch.op("scalar", (lambda tp_: lambda e: e.activation(
                stat["rs3"][:, tp_:tp_ + 1], stat["m23"][:, tp_:tp_ + 1],
                ACT.Ln))(tp_), (tvar,), "S")
            trs3 = sch.op("scalar", (lambda tp_: lambda e: e.activation(
                stat["rs3"][:, tp_:tp_ + 1], stat["rs3"][:, tp_:tp_ + 1],
                ACT.Exp, scale=-0.5))(tp_), (trr,), "S")
            tnmr3 = sch.op("vector", (lambda tp_: lambda e: e.scalar_tensor_tensor(
                stat["nmr3"][:, tp_:tp_ + 1], stat["mu3"][:, tp_:tp_ + 1], -1.0,
                stat["rs3"][:, tp_:tp_ + 1], OP.mult, OP.mult))(tp_),
                (trs3, tmua), "V")
            tout = sch.op("scalar", (lambda tp_: lambda e: e.activation(
                out_t[:], out_t[:], ACT.Identity,
                bias=stat["nmr3"][:, tp_:tp_ + 1],
                scale=stat["rs3"][:, tp_:tp_ + 1]))(tp_), (trs3, tnmr3), "S")
            tdma = sch.op("sync", (lambda tp_: lambda e: e.dma_start(
                out=out_d[tp_ * TILE:(tp_ + 1) * TILE, :], in_=out_t[:]))(tp_),
                (tout,), "DS")
            free["outb"] = tdma
            tok_out_dma.append(tdma)

    final = sch.barrier()

    with nc.Block() as block:
        @block.sync
        def _(e):
            sch.replay("sync", e, sems)
            for s, v in final:
                e.wait_ge(sems[s], v)

        @block.tensor
        def _(e):
            sch.replay("tensor", e, sems)

        @block.vector
        def _(e):
            sch.replay("vector", e, sems)

        @block.scalar
        def _(e):
            sch.replay("scalar", e, sems)

        @block.gpsimd
        def _(e):
            sch.replay("gpsimd", e, sems)

    ctx.close()
    return nc


_cache = {}


def kernel(L0_attn_x, L0_attn_W_in, L0_attn_b_in, L0_attn_k0, L0_attn_k1,
           L0_attn_gamma, L0_attn_beta, L0_attn_W_out, L0_attn_b_out,
           L0_ffn_norm1_w, L0_ffn_w1, L0_ffn_act_scale, L0_ffn_act_bias,
           L0_ffn_norm2_w, L0_ffn_w2):
    x = np.ascontiguousarray(np.asarray(L0_attn_x, dtype=np.float32))
    B, T, _ = x.shape
    consts = _host_consts()
    nc = build_nc()

    shared = {
        "W_in": np.ascontiguousarray(np.asarray(L0_attn_W_in, np.float32)),
        "b_in": np.ascontiguousarray(np.asarray(L0_attn_b_in, np.float32)).reshape(1, OPROJ),
        "k0_t": np.ascontiguousarray(np.asarray(L0_attn_k0, np.float32)).reshape(D, KS),
        "k1_t": np.ascontiguousarray(np.asarray(L0_attn_k1, np.float32)).reshape(D, KS),
        "gamma_t": np.ascontiguousarray(
            np.asarray(L0_attn_gamma, np.float32).reshape(8, 128).T),
        "beta_t": np.ascontiguousarray(
            np.asarray(L0_attn_beta, np.float32).reshape(8, 128).T),
        "W_out": np.ascontiguousarray(np.asarray(L0_attn_W_out, np.float32)),
        "b_out": np.ascontiguousarray(np.asarray(L0_attn_b_out, np.float32)).reshape(1, D),
        "nw1_r": np.ascontiguousarray(np.asarray(L0_ffn_norm1_w, np.float32)).reshape(1, D),
        "w1": np.ascontiguousarray(np.asarray(L0_ffn_w1, np.float32)),
        "as_i": np.ascontiguousarray(np.asarray(L0_ffn_act_scale, np.float32)).reshape(1, 1),
        "ab_i": np.ascontiguousarray(np.asarray(L0_ffn_act_bias, np.float32)).reshape(1, 1),
        "nw2_r": np.ascontiguousarray(np.asarray(L0_ffn_norm2_w, np.float32)).reshape(1, HID),
        "w2": np.ascontiguousarray(np.asarray(L0_ffn_w2, np.float32)),
    }
    for cn, _s, _sd, cdd in _CONST_SPECS:
        arrn = consts[cn]
        shared["c_" + cn] = arrn

    zeros_halo = np.zeros((NHALO * TILE, D), np.float32)
    hm0 = np.zeros((128, NHALO), np.float32)
    hm1 = np.ones((128, NHALO), np.float32)

    in_maps = []
    for core in range(8):
        b, h = core // 2, core % 2
        s = h * (NT * TILE)
        m = dict(shared)
        m["x_own"] = np.ascontiguousarray(x[b, s:s + NT * TILE])
        if h == 0:
            m["x_halo"] = zeros_halo
            m["hm"] = hm0
        else:
            m["x_halo"] = np.ascontiguousarray(x[b, s - NHALO * TILE:s])
            m["hm"] = hm1
        in_maps.append(m)

    res = run_bass_kernel_spmd(nc, in_maps, core_ids=list(range(8)))
    out = np.empty((B, T, D), res.results[0]["out"].dtype)
    for core in range(8):
        b, h = core // 2, core % 2
        s = h * (NT * TILE)
        out[b, s:s + NT * TILE] = res.results[core]["out"]
    return out


# version 35
# speedup vs baseline: 1.0586x; 1.0255x over previous
import numpy as np
import ml_dtypes
import concourse.bass as bass
import concourse.mybir as mybir
from concourse.bass_utils import run_bass_kernel_spmd

F32 = mybir.dt.float32
BF16 = mybir.dt.bfloat16
I32 = mybir.dt.int32
U8 = mybir.dt.uint8
OP = mybir.AluOpType
ACT = mybir.ActivationFunctionType
AX = mybir.AxisListType.X
BFnp = ml_dtypes.bfloat16

D = 1024
OPROJ = 3072
HID = 4096
KS = 128
TILE = 128
NHALO = 2
NT = 16
NALL = NHALO + NT
NW1T = HID // TILE
NW2T = D // TILE
NWELEM = float(HID * D)

_TAPS = ()
_LOCKSTEP = False

_TRIG_NAMES = ("wf1a", "wf1b", "wf2a", "wf2b", "winv1", "nwinv1", "winv2",
               "mc", "mi", "mb")


def _phase_pair(phase_int, sgn):
    P = np.mod(phase_int, 256).astype(np.int64)
    S = np.asarray(sgn, np.float64).copy()
    quarter = (P % 64 == 0)
    v0 = np.select([P == 0, P == 64, P == 128, P == 192], [0.0, 1.0, 0.0, -1.0])
    S = np.where(quarter, S * v0, S)
    P = np.where(quarter, 64, P)
    return P.astype(np.float32), S.astype(np.float32)


def _host_consts():
    N = 256
    t = np.arange(N)[:, None]
    f = np.arange(128)[None, :]
    c = {}

    def put(name, phase, sgn):
        P, S = _phase_pair(phase, np.broadcast_to(np.asarray(sgn, np.float64),
                                                  P0.shape))
        c["P_" + name] = P
        c["SGN_" + name] = S

    P0 = np.zeros((128, 128))
    put("wf1a", t[:128] * f + 64, 1.0)
    put("wf1b", t[128:] * f + 64, 1.0)
    ph = t * f
    ph[:, 0:1] = 128 * t + 64
    sg = np.full((N, 128), -1.0)
    sg[:, 0] = 1.0
    put("wf2a", ph[:128], sg[:128])
    put("wf2b", ph[128:], sg[128:])
    fr = np.arange(128)[:, None]
    j = np.arange(128, 256)[None, :]
    ph1 = fr * j + 64
    sg1 = np.where(fr == 0, 1.0 / N, 2.0 / N) * np.ones_like(j, dtype=np.float64)
    put("winv1", ph1, sg1)
    put("nwinv1", ph1, -sg1)
    ph2 = fr * j
    ph2[0:1, :] = 128 * j + 64
    sg2 = np.full((128, 128), -2.0 / N)
    sg2[0, :] = 1.0 / N
    put("winv2", ph2, sg2)
    kk = np.arange(KS)[:, None]
    put("mc", kk * f + 64, 1.0)
    phi = kk * f
    sgi = np.full((KS, 128), -1.0)
    sgi[:, 0] = 0.0
    put("mi", phi, sgi)
    phb = kk * f + 64
    phb[:, 0:1] = 128 * kk + 64
    put("mb", phb, 1.0)

    c["ident_bf"] = np.eye(128, dtype=BFnp)
    c["ident_f32"] = np.eye(128, dtype=np.float32)
    c["ones_row_bf"] = np.ones((1, 128), BFnp)
    c["ones_row_f32"] = np.ones((1, 128), np.float32)
    c["ones_col_f32"] = np.ones((128, 1), np.float32)
    c["ones_sq_f32"] = np.ones((128, 128), np.float32)
    return c

_CONST_SPECS = (
    [("P_" + n, [128, 128], F32, F32) for n in _TRIG_NAMES]
    + [("SGN_" + n, [128, 128], F32, F32) for n in _TRIG_NAMES]
    + [
        ("ident_bf", [128, 128], BF16, BF16), ("ident_f32", [128, 128], F32, F32),
        ("ones_row_bf", [1, 128], BF16, BF16), ("ones_row_f32", [1, 128], F32, F32),
        ("ones_col_f32", [128, 1], F32, F32), ("ones_sq_f32", [128, 128], F32, F32),
    ])
_TRIG_SBUF_DT = {"mc": F32, "mi": F32, "mb": F32}

_ENGS = ("tensor", "vector", "scalar", "gpsimd", "sync")
_DMA_LANES = {"DS": 8, "DQ": 4}
_SEMS = (("T", "V", "S", "Q")
         + tuple(f"DS{i}" for i in range(_DMA_LANES["DS"]))
         + tuple(f"DQ{i}" for i in range(_DMA_LANES["DQ"])))
_OWNER = {"T": "tensor", "V": "vector", "S": "scalar", "Q": "gpsimd"}


class Sch:
    def __init__(self):
        self.ops = {e: [] for e in _ENGS}
        self.cnt = {s: 0 for s in _SEMS}
        self.dma_rr = {"DS": 0, "DQ": 0}

    @staticmethod
    def _flat(deps):
        out = []
        def rec(d):
            if not d:
                return
            if isinstance(d, tuple) and len(d) == 2 and isinstance(d[0], str):
                out.append(d)
                return
            for x in d:
                rec(x)
        rec(tuple(deps))
        return tuple(out)

    def op(self, eng, fn, deps=(), inc=None):
        deps = self._flat(deps)
        if _LOCKSTEP:
            deps = self.barrier()
            if inc is None:
                inc = {"tensor": "T", "vector": "V", "scalar": "S",
                       "gpsimd": "Q", "sync": None}[eng]
        tok = None
        if inc:
            if inc in _DMA_LANES:
                lane = self.dma_rr[inc]
                self.dma_rr[inc] = (lane + 1) % _DMA_LANES[inc]
                sem = f"{inc}{lane}"
                if self.cnt[sem] > 0:
                    deps = deps + ((sem, self.cnt[sem]),)
                self.cnt[sem] += 16
                tok = (sem, self.cnt[sem])
                self.ops[eng].append((fn, deps, sem))
                return tok
            self.cnt[inc] += 1
            tok = (inc, self.cnt[inc])
        self.ops[eng].append((fn, deps, inc))
        return tok

    def barrier(self):
        return tuple((s, v) for s, v in self.cnt.items() if v > 0)

    def replay(self, eng_name, e, sems):
        waited = {}
        for fn, deps, inc in self.ops[eng_name]:
            for s, v in deps:
                if _OWNER.get(s) == eng_name:
                    continue
                if waited.get(s, 0) < v:
                    e.wait_ge(sems[s], v)
                    waited[s] = v
            inst = fn(e)
            if inc:
                assert inst is not None
                if inc.startswith("D"):
                    inst.then_inc(sems[inc], 16)
                else:
                    e.maybe_drain_then_inc((sems[inc], 1), fusable=True)


class Arena:

    def __init__(self, nc):
        self.nc = nc
        self.base = (nc.sbuf_base + 31) // 32 * 32
        self.total = nc.sbuf_top - self.base
        self.cur = 0
        self.marks = []

    def alloc(self, name, shape, dtype):
        per_part = int(np.prod(shape[1:])) * mybir.dt.size(dtype)
        off = (self.cur + 31) // 32 * 32
        self.cur = off + per_part
        assert self.cur <= self.total, \
            f"SBUF arena overflow at {name}: {self.cur} > {self.total}"
        return self.nc.alloc_sbuf_tensor_at(name, shape, dtype,
                                            offset=self.base + off)

    def push(self):
        self.marks.append(self.cur)

    def pop(self):
        self.cur = self.marks.pop()


def build_nc():
    nc = bass.Bass("TRN2", target_bir_lowering=False)

    dram = {}
    def din(name, shape, dtype=F32):
        dram[name] = nc.dram_tensor(name, shape, dtype, kind="ExternalInput").ap()
        return dram[name]

    x_own = din("x_own", [NT * TILE, D])
    x_halo = din("x_halo", [NHALO * TILE, D])
    hm = din("hm", [128, NHALO])
    W_in = din("W_in", [D, OPROJ])
    b_in = din("b_in", [1, OPROJ])
    k0_t = din("k0_t", [D, KS])
    k1_t = din("k1_t", [D, KS])
    gamma_t = din("gamma_t", [128, 8])
    beta_t = din("beta_t", [128, 8])
    W_out = din("W_out", [D, D])
    b_out = din("b_out", [1, D])
    nw1_r = din("nw1_r", [1, D])
    w1 = din("w1", [HID, D])
    as_i = din("as_i", [1, 1])
    ab_i = din("ab_i", [1, 1])
    nw2_r = din("nw2_r", [1, HID])
    w2 = din("w2", [D, HID])
    for cn, cs, _sd, cdd in _CONST_SPECS:
        din("c_" + cn, cs, cdd)

    out_d = nc.dram_tensor("out", [NT * TILE, D], F32, kind="ExternalOutput").ap()
    taps_d = {}
    for tname, _bk, tshape, tdt in _TAPS:
        taps_d[tname] = nc.dram_tensor(tname, tshape, tdt, kind="ExternalOutput").ap()

    sc_xp = nc.dram_tensor("sc_xp", [NT * TILE, D], BF16, kind="Internal").ap()
    sc_wq1 = nc.dram_tensor("sc_wq1", [HID, D], BF16, kind="Internal").ap()
    sc_wq2 = nc.dram_tensor("sc_wq2", [D, HID], BF16, kind="Internal").ap()

    ar = Arena(nc)
    A = ar.alloc

    cst = {}
    for cn, cs, csd, _dd in _CONST_SPECS:
        if cn.startswith("P_") or cn.startswith("SGN_"):
            continue
        cst[cn] = A("s_" + cn, cs, csd)
    for tn in _TRIG_NAMES:
        cst[tn] = A("s_" + tn, [128, 128], _TRIG_SBUF_DT.get(tn, BF16))
    hm_s = A("hm_s", [128, NHALO], F32)
    epsc = A("epsc", [128, 2], F32)
    asab = A("asab", [128, 2], F32)
    mc1b = A("mc1b", [128, 1], F32)
    mc2b = A("mc2b", [128, 1], F32)
    thr_row = A("thr_row", [1, 8], F32)
    wsum1 = A("wsum1", [128, NW1T], F32)
    wsum2 = A("wsum2", [128, NW2T], F32)
    wtot = A("wtot", [128, 2], F32)
    stat = {n: A("st_" + n, [128, 20], F32) for n in
            ("mu1", "m21", "rs1", "nmr1", "mu2", "m22", "rs2", "nmr2",
             "ssq", "amx1", "m1", "k1", "ss2", "amx2", "m2", "k2",
             "mu3", "m23", "rs3", "nmr3", "tmpa", "tmpb", "tmpc")}
    xp_t = [A(f"xp{i}", [128, D], BF16) for i in range(3)]
    out_t = A("outb", [128, D], F32)

    ar.push()
    Wg = A("Wg", [128, 8 * OPROJ], BF16)
    Wo = A("Wo", [128, 8 * D], BF16)
    Ktab = {k_: A("Ktab_" + k_, [128, D], BF16)
            for k_ in ("K0C", "K0I", "K0B", "K1C", "K1I", "K1B")}
    b_st = A("b_st", [1, OPROJ + D], BF16)
    b_eff = A("b_eff", [1, OPROJ], BF16)
    xr = [A(f"xr{i}", [128, D], F32) for i in range(4)]
    xn = [A(f"xn{i}", [128, D], BF16) for i in range(2)]
    xnT = [A(f"xnT{i}", [128, D], BF16) for i in range(2)]
    sq_sink = A("sq_sink", [128, D], F32)
    r_buf = A("r_buf", [128, D], F32)
    ar.push()
    wf32 = [A(f"wf32{i}", [128, OPROJ], F32) for i in range(2)]
    kT_b = A("kT_b", [128, D], F32)
    ar.pop()
    proj = [A(f"proj{i}", [128, OPROJ], BF16) for i in range(3)]
    y1 = [A(f"y1_{i}", [128, D], BF16) for i in range(3)]
    y2 = [A("y2_0", [128, D], BF16)]
    y2T = [A("y2T0", [128, D], BF16)]
    Pb = [A(f"Pb{i}", [128, 1024], BF16) for i in range(4)]
    a_peak = ar.cur
    ar.pop()

    ar.push()
    wqf = [A(f"wqf{i}", [128, HID], F32) for i in range(2)]
    wqb = [A(f"wqb{i}", [128, HID], BF16) for i in range(2)]
    wgt = A("wgt", [128, HID], BF16)
    totb = A("totb", [128, 1], F32)
    thrw = A("thrw", [128, 1], F32)
    nthrw = A("nthrw", [128, 1], F32)
    w_peak = ar.cur
    ar.pop()

    ar.push()
    wq1T = [A(f"wq1T{i}", [128, HID], BF16) for i in range(8)]
    wq2Tc = [A(f"wq2T{i}", [128, D], BF16) for i in range(32)]
    nw1r = A("nw1r", [128, D], BF16)
    nw2r = A("nw2r", [128, HID], BF16)
    t1 = A("t1", [128, D], F32)
    xq1 = A("xq1", [128, D], BF16)
    xq1T = A("xq1T", [128, D], BF16)
    r1s = A("r1s", [128, 1024], F32)
    t2f = A("t2f", [128, HID], F32)
    it2s = [A(f"it2s{i}", [128, 1024], I32) for i in range(2)]
    f2s = [A("f2s0", [128, 1024], BF16)]
    g2s = [A("g2s0", [128, 1024], BF16)]
    d2s = [A("d2s0", [128, 1024], BF16)]
    xq2o = [A(f"xq2o{i}", [128, 1024], BF16) for i in range(3)]
    xq2Tc = [A(f"xq2Tc{i}", [128, 1024], BF16) for i in range(2)]
    b_peak = ar.cur
    ar.pop()

    sch = Sch()

    psum_specs = [("ps_tr", [128, 1024], BF16), ("pj", [128, 1024], F32),
                  ("yc1", [128, 1024], F32), ("yc2", [128, 1024], F32),
                  ("ps_g", [128, 512], F32)]

    from contextlib import ExitStack
    ctx = ExitStack()
    ps = {n: ctx.enter_context(nc.psum_tensor(n, s, d)) for n, s, d in psum_specs}
    sems = {s: ctx.enter_context(nc.semaphore("sem_" + s)) for s in _SEMS}

    for _sn, _st in stat.items():
        sch.op("vector", (lambda st_: lambda e: e.memset(st_[:], 0.0))(_st), (), "V")
    tok_eps = sch.op("vector", lambda e: e.memset(epsc[:, 0:1], 1e-5), (), "V")
    tok_eps = sch.op("vector", lambda e: e.memset(epsc[:, 1:2], 1e-8), (tok_eps,), "V")

    free = {}
    tap_tok = {}

    def F(key):
        a, b = free.get(key), tap_tok.get(key)
        if b is None:
            return a
        return (a, b) if a is not None else b

    def pshalf(name, h):
        return ps[name][:, h * 512:(h + 1) * 512]

    def Fb(name):
        return (F(name + "0"), F(name + "1"))

    def setfree(name, tok):
        free[name + "0"] = tok
        free[name + "1"] = tok

    tok_cst = {}
    for cn, _s, csd, cdd in _CONST_SPECS:
        if cn.startswith("P_") or cn.startswith("SGN_"):
            continue
        if csd != cdd:
            tok_cst[cn] = sch.op("gpsimd", (lambda cn: lambda e: e.dma_start(
                out=cst[cn][:], in_=dram["c_" + cn]))(cn), (), "DQ")
        else:
            tok_cst[cn] = sch.op("sync", (lambda cn: lambda e: e.dma_start(
                out=cst[cn][:], in_=dram["c_" + cn]))(cn), (), "DS")

    _tp = 2.0 * np.pi
    _chi = float(np.float32(_tp / 256.0))
    _clo = float(np.float32(_tp / 256.0 - np.float64(_chi)))
    _thi = float(np.float32(_tp))
    _tlo = float(np.float32(_tp - np.float64(_thi)))
    _sin_set = None
    _main_set = None
    from concourse.hw_specs import get_activation_tables
    for _i, _funcs in enumerate(get_activation_tables(nc.m.arch).values()):
        fs = set(_funcs)
        if _sin_set is None and ACT.Sin in fs:
            _sin_set = _i
        if _main_set is None and {ACT.Identity, ACT.Square, ACT.Relu,
                                  ACT.Ln, ACT.Exp}.issubset(fs):
            _main_set = _i
    assert _sin_set is not None and _main_set is not None
    sch.op("scalar", (lambda sid: lambda e: e.add_instruction(
        mybir.InstLoadActFuncSet(
            name=nc.get_next_instruction_name(), act_func_set_id=sid,
            ins=[], outs=[])))(_sin_set), (), None)
    stage = wf32[0]
    sP, sG, sA, sB, sC = (stage[:, 0:128], stage[:, 128:256], stage[:, 256:384],
                          stage[:, 384:512], stage[:, 512:640])
    prev_gen = None
    for tn in _TRIG_NAMES:
        tokP = sch.op("sync", (lambda tn: lambda e: e.dma_start(
            out=sP, in_=dram["c_P_" + tn]))(tn), (prev_gen,), "DS")
        tokS = sch.op("sync", (lambda tn: lambda e: e.dma_start(
            out=sG, in_=dram["c_SGN_" + tn]))(tn), (prev_gen,), "DS")
        tg1 = sch.op("vector", (lambda: lambda e: e.tensor_scalar(
            sA, sP, _chi, None, OP.mult))(), (tokP,), "V")
        tg2 = sch.op("vector", (lambda: lambda e: e.scalar_tensor_tensor(
            sB, sP, _clo, sA, OP.mult, OP.add))(), (tg1,), "V")
        tg3 = sch.op("vector", (lambda: lambda e: e.tensor_scalar(
            sA, sP, 128.5, None, OP.is_gt))(), (tg2,), "V")
        tg4 = sch.op("vector", (lambda: lambda e: e.scalar_tensor_tensor(
            sC, sA, -_thi, sB, OP.mult, OP.add))(), (tg3,), "V")
        tg5 = sch.op("vector", (lambda: lambda e: e.scalar_tensor_tensor(
            sB, sA, -_tlo, sC, OP.mult, OP.add))(), (tg4,), "V")
        tg6 = sch.op("scalar", (lambda: lambda e: e.activation(
            sC, sB, ACT.Sin))(), (tg5,), "S")
        prev_gen = sch.op("vector", (lambda tn: lambda e: e.tensor_tensor(
            cst[tn][:], sC, sG, OP.mult))(tn), (tg6, tokS), "V")
        tok_cst[tn] = prev_gen
    sch.op("scalar", (lambda sid: lambda e: e.add_instruction(
        mybir.InstLoadActFuncSet(
            name=nc.get_next_instruction_name(), act_func_set_id=sid,
            ins=[], outs=[])))(_main_set), (prev_gen,), None)
    tok_hm = sch.op("sync", lambda e: e.dma_start(out=hm_s[:], in_=hm), (), "DS")
    tok_bin = sch.op("sync", lambda e: e.dma_start(
        out=thr_row[0:1, 0:1], in_=as_i), (), "DS")
    tok_abin = sch.op("sync", lambda e: e.dma_start(
        out=thr_row[0:1, 1:2], in_=ab_i), (), "DS")
    tok_ab_mm = sch.op("tensor", lambda e: e.matmul(
        ps["ps_g"][:, 0:2], cst["ones_row_f32"][:], thr_row[0:1, 0:2],
        start=True, stop=True), (tok_bin, tok_abin, tok_cst["ones_row_f32"]), "T")
    tok_asab = sch.op("vector", lambda e: e.tensor_copy(asab[:], ps["ps_g"][:, 0:2]),
                      (tok_ab_mm,), "V")
    free["ps_g"] = tok_asab

    tok_binrow = sch.op("gpsimd", lambda e: e.dma_start(
        out=b_st[0:1, 0:OPROJ], in_=b_in), (), "DQ")
    tok_boutrow = sch.op("gpsimd", lambda e: e.dma_start(
        out=b_st[0:1, OPROJ:OPROJ + D], in_=b_out), (), "DQ")

    tok_k = {}
    for ki, kd in ((0, k0_t), (1, k1_t)):
        tok_kload = sch.op("sync", (lambda kd: lambda e: e.dma_start(
            out=wf32[0][:, 0:8 * KS].rearrange("p (c k) -> p c k", c=8),
            in_=kd.rearrange("(c p) k -> p c k", p=128)))(kd),
            (F("wf32_0"),) if ki else (prev_gen,), "DS")
        tk_tr = None
        for c in range(8):
            tk_tr = sch.op("tensor", (lambda c: lambda e: e.transpose(
                ps["yc1"][:, c * 128:(c + 1) * 128], wf32[0][:, c * KS:(c + 1) * KS],
                cst["ident_f32"][:]))(c),
                (tok_kload, tok_cst["ident_f32"],
                 Fb("yc1") if c == 0 else None), "T")
        tok_kT = sch.op("vector", lambda e: e.tensor_copy(
            kT_b[:], ps["yc1"][:]), (tk_tr,), "V")
        setfree("yc1", tok_kT)
        free["wf32_0"] = tok_kT
        for tab, mat in (("C", "mc"), ("I", "mi"), ("B", "mb")):
            key = f"K{ki}{tab}"
            tmm = None
            for dc in range(2):
                tmm = sch.op("tensor", (lambda mat, dc: lambda e: e.matmul(
                    pshalf("pj", dc), cst[mat][:], kT_b[:, dc * 512:(dc + 1) * 512],
                    start=True, stop=True))(mat, dc),
                    (tok_kT, tok_cst[mat], F(f"pj{dc}")), "T" if dc == 1 else None)
            tev = sch.op("vector", (lambda key: lambda e: e.tensor_copy(
                Ktab[key][:], ps["pj"][:]))(key), (tmm,), "V")
            setfree("pj", tev)
        tok_k[ki] = tev

    bias_banks = [("pj", 0), ("pj", 1), ("yc1", 0), ("yc1", 1),
                  ("yc2", 0), ("yc2", 1)]
    gamma_s = stat["tmpa"]
    beta_s = stat["tmpb"]
    tok_gl = sch.op("sync", lambda e: e.dma_start(out=gamma_s[:, 0:8], in_=gamma_t), (), "DS")
    tok_bl = sch.op("sync", lambda e: e.dma_start(out=beta_s[:, 0:8], in_=beta_t), (), "DS")

    tok_wg_ev = {}
    tok_bias_mm = {}
    prev_wdma = [(), ()]
    for c in range(8):
        buf = c % 2
        tok_wdma = sch.op("sync", (lambda c, buf: lambda e: e.dma_start(
            out=wf32[buf][:], in_=W_in[c * 128:(c + 1) * 128, :]))(c, buf),
            prev_wdma[buf] + ((tok_k[1],) if c < 2 else ()), "DS")
        tok_wg_ev[c] = sch.op("vector", (lambda c, buf: lambda e: e.tensor_scalar(
            Wg[:, c * OPROJ:(c + 1) * OPROJ], wf32[buf][:],
            gamma_s[:, c:c + 1], None, OP.mult))(c, buf), (tok_wdma, tok_gl), "V")
        for oc in range(6):
            bn, bh = bias_banks[oc]
            tok_bias_mm[oc] = sch.op("tensor", (lambda c, buf, oc, bn, bh: lambda e:
                e.matmul(pshalf(bn, bh)[0:1, :], beta_s[:, c:c + 1],
                         wf32[buf][:, oc * 512:(oc + 1) * 512],
                         start=(c == 0), stop=(c == 7)))(c, buf, oc, bn, bh),
                (tok_wdma, tok_bl, F(f"{bn}{bh}") if c == 0 else None),
                "T" if oc == 5 else None)
        prev_wdma[buf] = (tok_wg_ev[c], tok_bias_mm[5])
    tok_beff = {}
    for oc in range(6):
        bn, bh = bias_banks[oc]
        tok_beff[oc] = sch.op("vector", (lambda oc, bn, bh: lambda e: e.tensor_tensor(
            b_eff[0:1, oc * 512:(oc + 1) * 512], pshalf(bn, bh)[0:1, :],
            b_st[0:1, oc * 512:(oc + 1) * 512], OP.add))(oc, bn, bh),
            (tok_bias_mm[5], tok_binrow), "V")
        free[f"{bn}{bh}"] = tok_beff[oc]
    tok_beff_all = tok_beff[5]

    tok_wo_all = []
    for c in range(8):
        tok_wo_all.append(sch.op("gpsimd", (lambda c: lambda e: e.dma_start(
            out=Wo[:, c * D:(c + 1) * D], in_=W_out[c * 128:(c + 1) * 128, :]))(c),
            (), "DQ"))
    tok_wo_all = tuple(tok_wo_all)

    conv_cst = tuple(tok_cst[n] for n in
                     ("wf1a", "wf1b", "wf2a", "wf2b", "winv1", "nwinv1", "winv2"))

    def emit_ln_stats(j, srcbuf, pre, eps, deps):
        mu, m2m, rs_, nmr = (stat["mu" + pre], stat["m2" + pre],
                             stat["rs" + pre], stat["nmr" + pre])
        tmu = sch.op("vector", (lambda: lambda e: e.reduce_sum(
            mu[:, j:j + 1], srcbuf[:], axis=AX))(), deps, "V")
        tm2 = sch.op("scalar", (lambda: lambda e: e.activation(
            sq_sink[:, 0:srcbuf.shape[-1]] if False else sq_sink[:],
            srcbuf[:], ACT.Square, accum_out=m2m[:, j:j + 1]))(), deps, "S")
        tmua = sch.op("scalar", (lambda: lambda e: e.activation(
            mu[:, j:j + 1], mu[:, j:j + 1], ACT.Identity, scale=1.0 / D))(), (tmu,), "S")
        tm2a = sch.op("scalar", (lambda: lambda e: e.activation(
            m2m[:, j:j + 1], m2m[:, j:j + 1], ACT.Identity, scale=1.0 / D,
            bias=epsc[:, 0:1]))(), (tm2, tok_eps), "S")
        tmu2 = sch.op("scalar", (lambda: lambda e: e.activation(
            stat["tmpc"][:, j:j + 1], mu[:, j:j + 1], ACT.Square))(), (tmua,), "S")
        tvar = sch.op("vector", (lambda: lambda e: e.tensor_tensor(
            m2m[:, j:j + 1], m2m[:, j:j + 1], stat["tmpc"][:, j:j + 1], OP.subtract))(),
            (tm2a, tmu2), "V")
        trr = sch.op("scalar", (lambda: lambda e: e.activation(
            rs_[:, j:j + 1], m2m[:, j:j + 1], ACT.Ln))(), (tvar,), "S")
        trs = sch.op("scalar", (lambda: lambda e: e.activation(
            rs_[:, j:j + 1], rs_[:, j:j + 1], ACT.Exp, scale=-0.5))(), (trr,), "S")
        tnmr = sch.op("vector", (lambda: lambda e: e.scalar_tensor_tensor(
            nmr[:, j:j + 1], mu[:, j:j + 1], -1.0, rs_[:, j:j + 1],
            OP.mult, OP.mult))(), (trs, tmua), "V")
        return trs, tnmr

    tok_x = [None] * NALL
    tok_xn = [None] * NALL
    tok_proj = [None] * NALL
    tok_y1 = [None] * NALL
    tok_y2 = [None] * NALL
    tok_r = [None] * NALL
    tok_xp_dma = [None] * NALL

    def tap(name, key_buf, deps, freekey=None):
        if not _TAPS:
            return
        for tname, bk, _s, _d in _TAPS:
            if bk == name:
                tk = sch.op("sync", (lambda tname, key_buf: lambda e: e.dma_start(
                    out=taps_d[tname], in_=key_buf[:]))(tname, key_buf), deps, "DS")
                if freekey:
                    p = tap_tok.get(freekey)
                    tap_tok[freekey] = tk if p is None else (p, tk)

    _PROJ_BANKS = ("pj", "yc1", "yc2")

    def emit_xload(j):
        xslot = j % 4
        src = x_halo[j * TILE:(j + 1) * TILE, :] if j < NHALO else \
            x_own[(j - NHALO) * TILE:(j - NHALO + 1) * TILE, :]
        tok_x[j] = sch.op("gpsimd", (lambda src, xslot: lambda e: e.dma_start(
            out=xr[xslot][:], in_=src))(src, xslot), (F(f"xr{xslot}"),), "DQ")

    def emit_ln1(j):
        xslot = j % 4
        trs, tnmr = emit_ln_stats(j, xr[xslot], "1", 1e-5, (tok_x[j],))
        xnb = j % 2
        tok_xn[j] = sch.op("scalar", (lambda j, xslot, xnb: lambda e: e.activation(
            xn[xnb][:], xr[xslot][:], ACT.Identity,
            bias=stat["nmr1"][:, j:j + 1], scale=stat["rs1"][:, j:j + 1]))(j, xslot, xnb),
            (trs, tnmr, F(f"xn{xnb}")), "S")
        if j < NHALO:
            free[f"xr{xslot}"] = tok_xn[j]

    def emit_front(j):
        xnb = j % 2
        ttr = None
        for c in range(8):
            ttr = sch.op("tensor", (lambda c, xnb: lambda e: e.transpose(
                ps["ps_tr"][:, c * 128:(c + 1) * 128],
                xn[xnb][:, c * 128:(c + 1) * 128], cst["ident_bf"][:]))(c, xnb),
                (tok_xn[j], F("ps_tr") if c == 0 else None,
                 tok_cst["ident_bf"] if j == 0 and c == 0 else None),
                "T" if c == 7 else None)
        tevt = sch.op("vector", (lambda xnb: lambda e: e.tensor_copy(
            xnT[xnb][:], ps["ps_tr"][:]))(xnb),
            (ttr, F(f"xnT{xnb}")), "V")
        free["ps_tr"] = tevt
        free[f"xn{xnb}"] = ttr
        trn = [tevt] * 8
        pslot = j % 3
        ev_last = None
        tmm = None
        for p in range(3):
            bankT = _PROJ_BANKS[p]
            for h in range(2):
                oc = 2 * p + h
                for c in range(8):
                    tmm = sch.op("tensor", (lambda oc, c, h, xnb, bankT: lambda e:
                        e.matmul(pshalf(bankT, h), xnT[xnb][:, c * 128:(c + 1) * 128],
                                 Wg[:, c * OPROJ + oc * 512: c * OPROJ + oc * 512 + 512],
                                 start=(c == 0), stop=False))(oc, c, h, xnb, bankT),
                        (trn[c], F(f"{bankT}{h}") if c == 0 else None,
                         tok_wg_ev[c] if j == 0 else None),
                        None)
                tmm = sch.op("tensor", (lambda oc, h, bankT: lambda e: e.matmul(
                    pshalf(bankT, h), cst["ones_row_bf"][:],
                    b_eff[0:1, oc * 512:oc * 512 + 512],
                    start=False, stop=True))(oc, h, bankT),
                    (tok_beff_all if j == 0 and p == 0 else None,),
                    "T" if h == 1 else None)
            if j < NHALO and p == 0:
                tev = sch.op("vector", (lambda j, p, pslot, bankT: lambda e:
                    e.tensor_scalar(proj[pslot][:, p * 1024:(p + 1) * 1024],
                                    ps[bankT][:], hm_s[:, j:j + 1], None,
                                    OP.mult))(j, p, pslot, bankT),
                    (tmm, tok_hm, F(f"proj{pslot}")), "V")
            else:
                tev = sch.op("vector", (lambda p, pslot, bankT: lambda e:
                    e.tensor_copy(proj[pslot][:, p * 1024:(p + 1) * 1024],
                                  ps[bankT][:]))(p, pslot, bankT),
                    (tmm, F(f"proj{pslot}")), "V")
            setfree(bankT, tev)
            ev_last = tev
        tok_proj[j] = ev_last
        free[f"xnT{xnb}"] = tmm

    def conv_hop(jj, src_bufs, src_toks, gate_buf, gate_off, gate_tok,
                 Kpre, dst_buf, dst_free_key, extra=()):
        tf1 = tf2 = None
        for dc in range(2):
            tf1 = sch.op("tensor", (lambda dc: lambda e: e.matmul(
                pshalf("yc1", dc), cst["wf1a"][:],
                src_bufs[0][:, dc * 512:(dc + 1) * 512],
                start=True, stop=False))(dc),
                (src_toks[0], F(f"yc1{dc}")) + (extra if dc == 0 else ()), None)
            tf1 = sch.op("tensor", (lambda dc: lambda e: e.matmul(
                pshalf("yc1", dc), cst["wf1b"][:],
                src_bufs[1][:, dc * 512:(dc + 1) * 512],
                start=False, stop=True))(dc), (src_toks[1],),
                "T" if dc == 1 else None)
        for dc in range(2):
            tf2 = sch.op("tensor", (lambda dc: lambda e: e.matmul(
                pshalf("yc2", dc), cst["wf2a"][:],
                src_bufs[0][:, dc * 512:(dc + 1) * 512],
                start=True, stop=False))(dc), (F(f"yc2{dc}"),), None)
            tf2 = sch.op("tensor", (lambda dc: lambda e: e.matmul(
                pshalf("yc2", dc), cst["wf2b"][:],
                src_bufs[1][:, dc * 512:(dc + 1) * 512],
                start=False, stop=True))(dc), (),
                "T" if dc == 1 else None)
        tp = []
        for pi, (bank, tabk) in enumerate(
                (("yc1", Kpre + "C"), ("yc1", Kpre + "I"),
                 ("yc2", Kpre + "I"), ("yc2", Kpre + "B"))):
            tdep = tf1 if bank == "yc1" else tf2
            tp.append(sch.op("vector", (lambda pi, bank, tabk: lambda e:
                e.tensor_tensor(Pb[pi][:], ps[bank][:], Ktab[tabk][:],
                                OP.mult))(pi, bank, tabk),
                (tdep, F(f"Pb{pi}")), "V"))
        setfree("yc1", tp[1])
        setfree("yc2", tp[3])
        tiv = None
        for dc in range(2):
            for mi_, (wkey, pi) in enumerate(
                    (("winv1", 0), ("nwinv1", 2), ("winv2", 1), ("winv2", 3))):
                tiv = sch.op("tensor", (lambda wkey, pi, dc, mi_: lambda e: e.matmul(
                    pshalf("pj", dc), cst[wkey][:],
                    Pb[pi][:, dc * 512:(dc + 1) * 512],
                    start=(mi_ == 0), stop=(mi_ == 3)))(wkey, pi, dc, mi_),
                    (tp[pi], F(f"pj{dc}")) if mi_ == 0 else (tp[pi],),
                    "T" if (dc == 1 and mi_ == 3) else None)
        for pi in range(4):
            free[f"Pb{pi}"] = tiv
        tg = sch.op("vector", (lambda gate_off: lambda e: e.tensor_tensor(
            dst_buf[:], ps["pj"][:],
            gate_buf[:, gate_off:gate_off + 1024], OP.mult))(gate_off),
            (tiv, gate_tok, F(dst_free_key)), "V")
        setfree("pj", tg)
        return tg

    def emit_conv1(j):
        pprev, pcur = (j - 1) % 3, j % 3
        y1slot = j % 3
        tok_y1[j] = conv_hop(
            j, (proj[pprev], proj[pcur]), (tok_proj[j - 1], tok_proj[j]),
            proj[pcur], D, tok_proj[j], "K0", y1[y1slot], f"y1_{y1slot}",
            extra=conv_cst if j == 1 else ())

    def emit_conv2_wout(j):
        pcur = j % 3
        xslot = j % 4
        tok_y2[j] = conv_hop(
            j, (y1[(j - 1) % 3], y1[j % 3]), (tok_y1[j - 1], tok_y1[j]),
            proj[pcur], 2 * D, tok_proj[j], "K1", y2[0], "y2_0")
        free[f"y1_{(j - 1) % 3}"] = tok_y2[j]
        free[f"proj{pcur}"] = (free.get(f"proj{pcur}"), tok_y2[j])
        ttr2 = None
        for c in range(8):
            ttr2 = sch.op("tensor", (lambda c: lambda e: e.transpose(
                ps["ps_tr"][:, c * 128:(c + 1) * 128],
                y2[0][:, c * 128:(c + 1) * 128], cst["ident_bf"][:]))(c),
                (tok_y2[j], F("ps_tr") if c == 0 else None),
                "T" if c == 7 else None)
        tevt2 = sch.op("vector", (lambda: lambda e: e.tensor_copy(
            y2T[0][:], ps["ps_tr"][:]))(), (ttr2, F("y2T0")), "V")
        free["ps_tr"] = tevt2
        free["y2_0"] = ttr2
        trn2 = [tevt2] * 8
        tmm2 = None
        for oc in range(2):
            for c in range(8):
                tmm2 = sch.op("tensor", (lambda oc, c: lambda e: e.matmul(
                    pshalf("pj", oc), y2T[0][:, c * 128:(c + 1) * 128],
                    Wo[:, c * D + oc * 512: c * D + oc * 512 + 512],
                    start=(c == 0), stop=False))(oc, c),
                    (trn2[c], F(f"pj{oc}") if c == 0 else None)
                    + (tok_wo_all if j == 2 and oc == 0 and c == 0 else ()),
                    None)
            tmm2 = sch.op("tensor", (lambda oc: lambda e: e.matmul(
                pshalf("pj", oc), cst["ones_row_bf"][:],
                b_st[0:1, OPROJ + oc * 512:OPROJ + oc * 512 + 512],
                start=False, stop=True))(oc),
                (tok_boutrow if j == 2 and oc == 0 else None,),
                "T" if oc == 1 else None)
        ev_r = sch.op("vector", (lambda xslot: lambda e:
            e.scalar_tensor_tensor(r_buf[:], xr[xslot][:], 2.0,
                                   ps["pj"][:], OP.mult, OP.add))(xslot),
            (tmm2, tok_x[j], F("r_buf")), "V")
        setfree("pj", ev_r)
        tok_r[j] = ev_r
        free["y2T0"] = tmm2
        free[f"xr{xslot}"] = ev_r
        trs2, tnmr2 = emit_ln_stats(j, r_buf, "2", 1e-5, (tok_r[j],))
        xpb = j % 2
        tok_xp = sch.op("scalar", (lambda j, xpb: lambda e: e.activation(
            xp_t[xpb][:], r_buf[:], ACT.Identity,
            bias=stat["nmr2"][:, j:j + 1], scale=stat["rs2"][:, j:j + 1]))(j, xpb),
            (trs2, tnmr2, F(f"xp{xpb}")), "S")
        free["r_buf"] = tok_xp
        tok_xp_dma[j] = sch.op("gpsimd", (lambda j, xpb: lambda e: e.dma_start(
            out=sc_xp[(j - NHALO) * TILE:(j - NHALO + 1) * TILE, :],
            in_=xp_t[xpb][:]))(j, xpb), (tok_xp,), "DQ")
        free[f"xp{xpb}"] = tok_xp_dma[j]

    emit_xload(0)
    emit_ln1(0)
    emit_xload(1)
    for i in range(NALL + 2):
        if i < NALL:
            emit_front(i)
        if 1 <= i - 1 < NALL:
            emit_conv1(i - 1)
        if i + 1 < NALL:
            emit_ln1(i + 1)
        if 2 <= i - 2 < NALL:
            emit_conv2_wout(i - 2)
        if i + 2 < NALL:
            emit_xload(i + 2)

    barrierA = sch.barrier()

    def wquant(wdram, ntiles, width, wsum, scdram, mc_t, wtcol):
        toks_sum = []
        prev = [F("wqf0"), F("wqf1")]
        for i in range(ntiles):
            buf = i % 2
            td = sch.op("gpsimd", (lambda i, buf: lambda e: e.dma_start(
                out=wqf[buf][:, 0:width], in_=wdram[i * 128:(i + 1) * 128, :]))(i, buf),
                barrierA + ((prev[buf],) if prev[buf] else ()), "DQ")
            ts_ = sch.op("vector", (lambda i, buf: lambda e: e.reduce_sum(
                wsum[:, i:i + 1], wqf[buf][:, 0:width], axis=AX,
                apply_absolute_value=True))(i, buf), (td,), "V")
            prev[buf] = ts_
            toks_sum.append(ts_)
        trow = sch.op("vector", (lambda: lambda e: e.reduce_sum(
            wtot[:, wtcol:wtcol + 1], wsum[:, 0:ntiles], axis=AX))(),
            tuple(toks_sum), "V")
        tap(f"wsum_{wtcol}", wsum, (trow,))
        tap(f"wtot_{wtcol}", wtot, (trow,), freekey="wtot")
        tmm = sch.op("tensor", (lambda: lambda e: e.matmul(
            ps["ps_g"][:, 0:1], cst["ones_sq_f32"][:], wtot[:, wtcol:wtcol + 1],
            start=True, stop=True))(), (trow, F("ps_g"), tok_cst["ones_sq_f32"]), "T")
        tev = sch.op("vector", (lambda: lambda e: e.tensor_copy(
            totb[:], ps["ps_g"][:, 0:1]))(), (tmm, F("totb")), "V")
        free["ps_g"] = tev
        tap(f"totb_{wtcol}", totb, (tev,), freekey="totb")
        tmc = sch.op("vector", (lambda: lambda e: e.tensor_scalar(
            mc_t[:], totb[:], 1.0 / NWELEM, 1e-5, OP.mult, OP.max))(), (tev,), "V")
        free["totb"] = tmc
        tth = sch.op("vector", (lambda: lambda e: e.tensor_scalar(
            thrw[:], mc_t[:], 0.5, None, OP.mult))(), (tmc, F("thrw")), "V")
        tnt = sch.op("vector", (lambda: lambda e: e.tensor_scalar(
            nthrw[:], thrw[:], -1.0, None, OP.mult))(), (tth, F("nthrw")), "V")
        prev_d = [None, None]
        tlast = None
        tstores = []
        for i in range(ntiles):
            buf = i % 2
            td = sch.op("gpsimd", (lambda i, buf: lambda e: e.dma_start(
                out=wqf[buf][:, 0:width], in_=wdram[i * 128:(i + 1) * 128, :]))(i, buf),
                ((prev_d[buf],) if prev_d[buf] else (toks_sum[-1],)), "DQ")
            tg_ = sch.op("vector", (lambda i, buf: lambda e: e.tensor_scalar(
                wgt[:, 0:width], wqf[buf][:, 0:width], nthrw[:], None,
                OP.is_lt))(i, buf), (td, tnt, F("wgt")), "V")
            tq_ = sch.op("vector", (lambda i, buf: lambda e: e.scalar_tensor_tensor(
                wqb[buf][:, 0:width], wqf[buf][:, 0:width], thrw[:],
                wgt[:, 0:width], OP.is_gt, OP.subtract))(i, buf),
                (tg_, F(f"wqb{buf}")), "V")
            free["wgt"] = tq_
            prev_d[buf] = tq_
            tst = sch.op("sync", (lambda i, buf: lambda e: e.dma_start(
                out=scdram[i * 128:(i + 1) * 128, :], in_=wqb[buf][:, 0:width]))(i, buf),
                (tq_,), "DS")
            free[f"wqb{buf}"] = tst
            tlast = tst
            tstores.append(tst)
        free["thrw"] = tlast
        free["nthrw"] = tlast
        free["wqf0"] = prev_d[0]
        free["wqf1"] = prev_d[1]
        return tuple(tstores)

    tok_wq1 = wquant(w1, NW1T, D, wsum1, sc_wq1, mc1b, 0)
    tok_wq2 = wquant(w2, NW2T, HID, wsum2, sc_wq2, mc2b, 1)
    tap("mc1b", mc1b, tok_wq1)
    tap("mc2b", mc2b, tok_wq2)

    barrierW = sch.barrier()
    tok_wq1T = [None] * 8
    for dc in range(8):
        tok_wq1T[dc] = sch.op("sync", (lambda dc: lambda e: e.dma_start(
            out=wq1T[dc][:], in_=sc_wq1[:, dc * 128:(dc + 1) * 128],
            transpose=True))(dc), barrierW + tok_wq1, "DS")
    for dc in range(8):
        tap(f"wq1T{dc}", wq1T[dc], (tok_wq1T[dc],))
    tok_wq2T = [None] * 32
    for hc in range(32):
        tok_wq2T[hc] = sch.op("sync", (lambda hc: lambda e: e.dma_start(
            out=wq2Tc[hc][:], in_=sc_wq2[:, hc * 128:(hc + 1) * 128],
            transpose=True))(hc), barrierW + tok_wq2, "DS")

    tok_nw1row = sch.op("sync", lambda e: e.dma_start(
        out=t1[0:1, 0:D], in_=nw1_r), barrierW, "DS")
    tok_nw2row = sch.op("sync", lambda e: e.dma_start(
        out=t2f[0:1, 0:HID], in_=nw2_r), barrierW, "DS")
    tok_nwb = {}
    for oc in range(2):
        tm = sch.op("tensor", (lambda oc: lambda e: e.matmul(
            ps["ps_g"][:], cst["ones_row_f32"][:], t1[0:1, oc * 512:(oc + 1) * 512],
            start=True, stop=True))(oc), barrierW + (tok_nw1row, F("ps_g")), "T")
        tok_nwb[oc] = sch.op("vector", (lambda oc: lambda e: e.tensor_copy(
            nw1r[:, oc * 512:(oc + 1) * 512], ps["ps_g"][:]))(oc), (tm,), "V")
        free["ps_g"] = tok_nwb[oc]
    tok_nw1r = tok_nwb[1]
    tok_nw1r_T = tm
    for oc in range(8):
        tm = sch.op("tensor", (lambda oc: lambda e: e.matmul(
            ps["ps_g"][:], cst["ones_row_f32"][:], t2f[0:1, oc * 512:(oc + 1) * 512],
            start=True, stop=True))(oc), (tok_nw2row, F("ps_g")), "T")
        tok_nwb[oc] = sch.op("vector", (lambda oc: lambda e: e.tensor_copy(
            nw2r[:, oc * 512:(oc + 1) * 512], ps["ps_g"][:]))(oc), (tm,), "V")
        free["ps_g"] = tok_nwb[oc]
    tok_nw2r = tok_nwb[7]
    free["t1"] = tok_nw1r_T
    for _k in range(4):
        free[f"t2f{_k}"] = tm

    def quant_round(dst_bf, src, m_col, itbuf, fbuf, gbuf, dbuf, width, deps,
                    levels=1):
        tcast = sch.op("scalar", (lambda: lambda e: e.activation(
            itbuf[:, 0:width], src[:, 0:width], ACT.Identity, scale=m_col))(),
            deps, "S")
        tf = sch.op("vector", (lambda: lambda e: e.scalar_tensor_tensor(
            fbuf[:, 0:width], src[:, 0:width], m_col, itbuf[:, 0:width],
            OP.mult, OP.subtract))(), (tcast,), "V")
        tabs = sch.op("scalar", (lambda: lambda e: e.activation(
            dbuf[:, 0:width], fbuf[:, 0:width], ACT.Abs))(), (tf,), "S")
        tsgn = sch.op("scalar", (lambda: lambda e: e.activation(
            fbuf[:, 0:width], fbuf[:, 0:width], ACT.Sign))(), (tabs,), "S")
        tn = sch.op("vector", (lambda: lambda e: e.tensor_scalar(
            gbuf[:, 0:width], dbuf[:, 0:width], 0.5, None, OP.is_gt))(), (tabs,), "V")
        for lv in range(1, levels):
            tn = sch.op("vector", (lambda lv: lambda e: e.scalar_tensor_tensor(
                gbuf[:, 0:width], dbuf[:, 0:width], 0.5 + lv, gbuf[:, 0:width],
                OP.is_gt, OP.add))(lv), (tn,), "V")
        td_ = sch.op("vector", (lambda: lambda e: e.tensor_tensor(
            dbuf[:, 0:width], fbuf[:, 0:width], gbuf[:, 0:width], OP.mult))(),
            (tn, tsgn), "V")
        tq_ = sch.op("vector", (lambda: lambda e: e.tensor_tensor(
            dst_bf[:, 0:width], itbuf[:, 0:width], dbuf[:, 0:width], OP.add))(),
            (td_,), "V")
        return tq_, tf

    def rms_quant_chain(t, pre, ss_src_tok, amax_tok, mcb, nelem):
        ssn, amx, mcol, kcol = (stat["ss" + pre] if ("ss" + pre) in stat else stat["ssq"],
                                stat["amx" + pre], stat["m" + pre], stat["k" + pre])
        ta = sch.op("scalar", (lambda: lambda e: e.activation(
            stat["tmpa"][:, t:t + 1], ssn[:, t:t + 1], ACT.Identity,
            scale=1.0 / nelem, bias=epsc[:, 1:2]))(), (ss_src_tok, tok_eps), "S")
        tb = sch.op("scalar", (lambda: lambda e: e.activation(
            stat["tmpb"][:, t:t + 1], stat["tmpa"][:, t:t + 1], ACT.Ln))(), (ta,), "S")
        trs = sch.op("scalar", (lambda: lambda e: e.activation(
            stat["tmpa"][:, t:t + 1], stat["tmpb"][:, t:t + 1], ACT.Exp,
            scale=-0.5))(), (tb,), "S")
        tc = sch.op("vector", (lambda: lambda e: e.tensor_tensor(
            stat["tmpc"][:, t:t + 1], amx[:, t:t + 1], stat["tmpa"][:, t:t + 1],
            OP.mult))(), (trs, amax_tok), "V")
        tc2 = sch.op("vector", (lambda: lambda e: e.tensor_scalar(
            stat["tmpc"][:, t:t + 1], stat["tmpc"][:, t:t + 1], 1e-5, None, OP.max))(),
            (tc,), "V")
        trc0 = sch.op("scalar", (lambda: lambda e: e.activation(
            stat["tmpb"][:, t:t + 1], stat["tmpc"][:, t:t + 1], ACT.Ln))(), (tc2,), "S")
        trc = sch.op("scalar", (lambda: lambda e: e.activation(
            stat["tmpb"][:, t:t + 1], stat["tmpb"][:, t:t + 1], ACT.Exp,
            scale=-1.0))(), (trc0,), "S")
        tm_ = sch.op("vector", (lambda: lambda e: e.scalar_tensor_tensor(
            mcol[:, t:t + 1], stat["tmpb"][:, t:t + 1], 127.0, stat["tmpa"][:, t:t + 1],
            OP.mult, OP.mult))(), (trc, F("stm" + pre)), "V")
        tk_ = sch.op("vector", (lambda: lambda e: e.scalar_tensor_tensor(
            kcol[:, t:t + 1], stat["tmpc"][:, t:t + 1], 1.0 / 127.0, mcb,
            OP.mult, OP.mult))(), (tc2, F("stk" + pre)), "V")
        return tm_, tk_

    tok_out_dma = []
    itp = [0]
    Fst = {}
    Sst = {}

    def emit_stageF(t):
        xpb = t % 3
        d = {"xpb": xpb}
        tok_xpl = sch.op("sync", (lambda t, xpb: lambda e: e.dma_start(
            out=xp_t[xpb][:], in_=sc_xp[t * TILE:(t + 1) * TILE, :]))(t, xpb),
            (F(f"xp{xpb}"), tok_xp_dma[t + NHALO]), "DS")
        tss = sch.op("scalar", (lambda t, xpb: lambda e: e.activation(
            t1[:], xp_t[xpb][:], ACT.Square,
            accum_out=stat["ssq"][:, t:t + 1]))(t, xpb),
            (tok_xpl, F("t1")), "S")
        tt1 = sch.op("vector", (lambda xpb: lambda e: e.tensor_tensor(
            t1[:], xp_t[xpb][:], nw1r[:], OP.mult))(xpb),
            (tss, tok_xpl, tok_nw1r), "V")
        tam = sch.op("vector", (lambda t: lambda e: e.reduce_max(
            stat["amx1"][:, t:t + 1], t1[:], axis=AX,
            apply_absolute_value=True))(t), (tt1,), "V")
        tm1, tk1 = rms_quant_chain(t, "1", tss, tam, mc1b[:], D)
        d["tk1"] = tk1
        tq1, _tf1 = quant_round(xq1, t1, stat["m1"][:, t:t + 1],
                                it2s[itp[0]], xq1, g2s[0], d2s[0], 1024,
                                (tm1, tt1, F("xq1"), F(f"it{itp[0]}")),
                                levels=3)
        free[f"it{itp[0]}"] = tq1
        itp[0] ^= 1
        free["t1"] = tq1
        trq1 = []
        for c in range(8):
            trq1.append(sch.op("sync", (lambda c: lambda e: e.dma_start(
                out=xq1T[:, c * 128:(c + 1) * 128],
                in_=xq1[:, c * 128:(c + 1) * 128], transpose=True))(c),
                (tq1, F("xq1T") if c == 0 else None), "DS"))
        d["trq1"] = trq1
        free["xq1"] = tuple(trq1)
        return d

    def emit_chunkSV(tp_, k):
        tq2, tf2 = quant_round(
            xq2o[k % 3], t2f[:, k * 1024:(k + 1) * 1024],
            stat["m2"][:, tp_:tp_ + 1], it2s[itp[0]], f2s[0], g2s[0],
            d2s[0], 1024,
            (Sst[tp_]["tok_t2"][k], Sst[tp_]["tm2"], F(f"it{itp[0]}"),
             F(f"xq2o{k % 3}")), levels=4)
        free[f"it{itp[0]}"] = tq2
        itp[0] ^= 1
        free[f"t2f{k}"] = tf2
        return tq2

    def emit_superstrip(t, ss):
        tmm = [None, None]
        for h in range(2):
            for dc in range(8):
                tmm[h] = sch.op("tensor",
                    (lambda ss, dc, h: lambda e: e.matmul(
                        pshalf("pj", h), xq1T[:, dc * 128:(dc + 1) * 128],
                        wq1T[dc][:, ss * 1024 + h * 512:
                                   ss * 1024 + (h + 1) * 512],
                        start=(dc == 0), stop=(dc == 7)))(ss, dc, h),
                    (Fst[t]["trq1"][dc],
                     F(f"pj{h}") if dc == 0 else None,
                     tok_wq1T[dc] if t == 0 else None),
                    "T" if dc == 7 else None)
        trel = sch.op("scalar", (lambda t: lambda e: e.activation(
            r1s[:], ps["pj"][:], ACT.Relu,
            scale=stat["k1"][:, t:t + 1]))(t),
            (tmm[0], tmm[1], Fst[t]["tk1"], F("r1s")), "S")
        setfree("pj", trel)
        tsq = sch.op("scalar", (lambda: lambda e: e.activation(
            r1s[:], r1s[:], ACT.Square))(), (trel,), "S")
        th2 = sch.op("scalar", (lambda ss: lambda e: e.activation(
            t2f[:, ss * 1024:(ss + 1) * 1024], r1s[:], ACT.Identity,
            scale=asab[:, 0:1], bias=asab[:, 1:2]))(ss),
            (tsq, tok_asab if t == 0 else None, F(f"t2f{ss}")), "S")
        free["r1s"] = th2
        if ss == 3:
            free["xq1T"] = tmm[1]
        return th2

    def emit_chunkT(tp_, k, tq2):
        kb = k % 2
        ko = k % 3
        trk = []
        for c in range(8):
            trk.append(sch.op("sync", (lambda c, kb, ko: lambda e: e.dma_start(
                out=xq2Tc[kb][:, c * 128:(c + 1) * 128],
                in_=xq2o[ko][:, c * 128:(c + 1) * 128],
                transpose=True))(c, kb, ko),
                (tq2, F(f"xq2Tc{kb}") if c == 0 else None), "DS"))
        tmm2 = None
        for oc in range(2):
            for q in range(8):
                tmm2 = sch.op("tensor",
                    (lambda k, q, oc, kb: lambda e: e.matmul(
                        pshalf("yc1", oc),
                        xq2Tc[kb][:, q * 128:(q + 1) * 128],
                        wq2Tc[k * 8 + q][:, oc * 512:(oc + 1) * 512],
                        start=(k == 0 and q == 0),
                        stop=(k == 3 and q == 7),
                        skip_group_check=True))(k, q, oc, kb),
                    (trk[q],
                     tok_wq2T[k * 8 + q] if tp_ == 0 and oc == 0 else None,
                     F(f"yc1{oc}") if (k == 0 and q == 0) else None),
                    "T" if (oc == 1 and q == 7) else None)
        free[f"xq2Tc{kb}"] = tmm2
        free[f"xq2o{ko}"] = tuple(trk)
        return tmm2

    def emit_stat_chunk(t, k, th2_tok):
        tss2 = sch.op("scalar", (lambda k: lambda e: e.activation(
            t1[:], t2f[:, k * 1024:(k + 1) * 1024], ACT.Square,
            accum_out=stat["ss2"][:, 16 + k:17 + k]))(k),
            (th2_tok, F("t1")), "S")
        tt2 = sch.op("vector", (lambda k: lambda e: e.tensor_tensor(
            t2f[:, k * 1024:(k + 1) * 1024],
            t2f[:, k * 1024:(k + 1) * 1024],
            nw2r[:, k * 1024:(k + 1) * 1024], OP.mult))(k),
            (tss2, tok_nw2r if t == 0 else None), "V")
        tam2 = sch.op("vector", (lambda k: lambda e: e.reduce_max(
            stat["amx2"][:, 16 + k:17 + k],
            t2f[:, k * 1024:(k + 1) * 1024], axis=AX,
            apply_absolute_value=True))(k), (tt2,), "V")
        return tss2, tt2, tam2

    def emit_stats(t, tok_sc):
        d = {}
        tok_ss2 = [x[0] for x in tok_sc]
        tok_t2 = [x[1] for x in tok_sc]
        tok_am2 = [x[2] for x in tok_sc]
        tssc = sch.op("vector", (lambda t: lambda e: e.reduce_sum(
            stat["ss2"][:, t:t + 1], stat["ss2"][:, 16:20], axis=AX))(t),
            tuple(tok_ss2), "V")
        tamc = sch.op("vector", (lambda t: lambda e: e.reduce_max(
            stat["amx2"][:, t:t + 1], stat["amx2"][:, 16:20], axis=AX))(t),
            tuple(tok_am2), "V")
        tm2, tk2 = rms_quant_chain(t, "2", tssc, tamc, mc2b[:], HID)
        d["tm2"], d["tk2"] = tm2, tk2
        d["tok_t2"] = tok_t2
        return d

    emit_taps = None

    Fst[0] = None
    for t in range(NT + 1):
        if t == 0:
            Fst[0] = emit_stageF(0)
        tok_sc = [None] * 4
        tq2s = {}
        tmm2_last = None
        for k in range(6):
            if t >= 1 and k < 4:
                tq2s[k] = emit_chunkSV(t - 1, k)
            if t < NT and 1 <= k <= 4:
                th2_tok = emit_superstrip(t, k - 1)
                tok_sc[k - 1] = emit_stat_chunk(t, k - 1, th2_tok)
            if t >= 1 and k >= 2:
                tmm2_last = emit_chunkT(t - 1, k - 2, tq2s[k - 2])
        if t < NT:
            Sst[t] = emit_stats(t, tok_sc)
        if t + 1 < NT:
            Fst[t + 1] = emit_stageF(t + 1)
        if t >= 1:
            tp_ = t - 1
            pxpb = tp_ % 3
            ev3 = sch.op("vector", (lambda tp_, pxpb: lambda e:
                e.scalar_tensor_tensor(out_t[:], ps["yc1"][:],
                                       stat["k2"][:, tp_:tp_ + 1],
                                       xp_t[pxpb][:], OP.mult, OP.add))(tp_, pxpb),
                (tmm2_last, Sst[tp_]["tk2"], F("outb")), "V")
            setfree("yc1", ev3)
            free[f"xp{pxpb}"] = ev3
        if t >= 1:
            tp_ = t - 1
            tmu3 = sch.op("vector", (lambda tp_: lambda e: e.reduce_sum(
                stat["mu3"][:, tp_:tp_ + 1], out_t[:], axis=AX))(tp_), (ev3,), "V")
            tm23 = sch.op("scalar", (lambda tp_: lambda e: e.activation(
                t1[:], out_t[:], ACT.Square,
                accum_out=stat["m23"][:, tp_:tp_ + 1]))(tp_),
                (ev3, F("t1")), "S")
            tmua = sch.op("scalar", (lambda tp_: lambda e: e.activation(
                stat["mu3"][:, tp_:tp_ + 1], stat["mu3"][:, tp_:tp_ + 1],
                ACT.Identity, scale=1.0 / D))(tp_), (tmu3,), "S")
            tm2a = sch.op("scalar", (lambda tp_: lambda e: e.activation(
                stat["m23"][:, tp_:tp_ + 1], stat["m23"][:, tp_:tp_ + 1],
                ACT.Identity, scale=1.0 / D, bias=epsc[:, 0:1]))(tp_),
                (tm23, tok_eps), "S")
            tmu2 = sch.op("scalar", (lambda tp_: lambda e: e.activation(
                stat["tmpc"][:, tp_:tp_ + 1], stat["mu3"][:, tp_:tp_ + 1],
                ACT.Square))(tp_), (tmua,), "S")
            tvar = sch.op("vector", (lambda tp_: lambda e: e.tensor_tensor(
                stat["m23"][:, tp_:tp_ + 1], stat["m23"][:, tp_:tp_ + 1],
                stat["tmpc"][:, tp_:tp_ + 1], OP.subtract))(tp_),
                (tm2a, tmu2), "V")
            trr = sch.op("scalar", (lambda tp_: lambda e: e.activation(
                stat["rs3"][:, tp_:tp_ + 1], stat["m23"][:, tp_:tp_ + 1],
                ACT.Ln))(tp_), (tvar,), "S")
            trs3 = sch.op("scalar", (lambda tp_: lambda e: e.activation(
                stat["rs3"][:, tp_:tp_ + 1], stat["rs3"][:, tp_:tp_ + 1],
                ACT.Exp, scale=-0.5))(tp_), (trr,), "S")
            tnmr3 = sch.op("vector", (lambda tp_: lambda e: e.scalar_tensor_tensor(
                stat["nmr3"][:, tp_:tp_ + 1], stat["mu3"][:, tp_:tp_ + 1], -1.0,
                stat["rs3"][:, tp_:tp_ + 1], OP.mult, OP.mult))(tp_),
                (trs3, tmua), "V")
            tout = sch.op("scalar", (lambda tp_: lambda e: e.activation(
                out_t[:], out_t[:], ACT.Identity,
                bias=stat["nmr3"][:, tp_:tp_ + 1],
                scale=stat["rs3"][:, tp_:tp_ + 1]))(tp_), (trs3, tnmr3), "S")
            tdma = sch.op("sync", (lambda tp_: lambda e: e.dma_start(
                out=out_d[tp_ * TILE:(tp_ + 1) * TILE, :], in_=out_t[:]))(tp_),
                (tout,), "DS")
            free["outb"] = tdma
            tok_out_dma.append(tdma)

    final = sch.barrier()

    with nc.Block() as block:
        @block.sync
        def _(e):
            sch.replay("sync", e, sems)
            for s, v in final:
                e.wait_ge(sems[s], v)

        @block.tensor
        def _(e):
            sch.replay("tensor", e, sems)

        @block.vector
        def _(e):
            sch.replay("vector", e, sems)

        @block.scalar
        def _(e):
            sch.replay("scalar", e, sems)

        @block.gpsimd
        def _(e):
            sch.replay("gpsimd", e, sems)

    ctx.close()
    return nc


_cache = {}


def kernel(L0_attn_x, L0_attn_W_in, L0_attn_b_in, L0_attn_k0, L0_attn_k1,
           L0_attn_gamma, L0_attn_beta, L0_attn_W_out, L0_attn_b_out,
           L0_ffn_norm1_w, L0_ffn_w1, L0_ffn_act_scale, L0_ffn_act_bias,
           L0_ffn_norm2_w, L0_ffn_w2):
    x = np.ascontiguousarray(np.asarray(L0_attn_x, dtype=np.float32))
    B, T, _ = x.shape
    consts = _host_consts()
    nc = build_nc()

    shared = {
        "W_in": np.ascontiguousarray(np.asarray(L0_attn_W_in, np.float32)),
        "b_in": np.ascontiguousarray(np.asarray(L0_attn_b_in, np.float32)).reshape(1, OPROJ),
        "k0_t": np.ascontiguousarray(np.asarray(L0_attn_k0, np.float32)).reshape(D, KS),
        "k1_t": np.ascontiguousarray(np.asarray(L0_attn_k1, np.float32)).reshape(D, KS),
        "gamma_t": np.ascontiguousarray(
            np.asarray(L0_attn_gamma, np.float32).reshape(8, 128).T),
        "beta_t": np.ascontiguousarray(
            np.asarray(L0_attn_beta, np.float32).reshape(8, 128).T),
        "W_out": np.ascontiguousarray(np.asarray(L0_attn_W_out, np.float32)),
        "b_out": np.ascontiguousarray(np.asarray(L0_attn_b_out, np.float32)).reshape(1, D),
        "nw1_r": np.ascontiguousarray(np.asarray(L0_ffn_norm1_w, np.float32)).reshape(1, D),
        "w1": np.ascontiguousarray(np.asarray(L0_ffn_w1, np.float32)),
        "as_i": np.ascontiguousarray(np.asarray(L0_ffn_act_scale, np.float32)).reshape(1, 1),
        "ab_i": np.ascontiguousarray(np.asarray(L0_ffn_act_bias, np.float32)).reshape(1, 1),
        "nw2_r": np.ascontiguousarray(np.asarray(L0_ffn_norm2_w, np.float32)).reshape(1, HID),
        "w2": np.ascontiguousarray(np.asarray(L0_ffn_w2, np.float32)),
    }
    for cn, _s, _sd, cdd in _CONST_SPECS:
        arrn = consts[cn]
        shared["c_" + cn] = arrn

    zeros_halo = np.zeros((NHALO * TILE, D), np.float32)
    hm0 = np.zeros((128, NHALO), np.float32)
    hm1 = np.ones((128, NHALO), np.float32)

    in_maps = []
    for core in range(8):
        b, h = core // 2, core % 2
        s = h * (NT * TILE)
        m = dict(shared)
        m["x_own"] = np.ascontiguousarray(x[b, s:s + NT * TILE])
        if h == 0:
            m["x_halo"] = zeros_halo
            m["hm"] = hm0
        else:
            m["x_halo"] = np.ascontiguousarray(x[b, s - NHALO * TILE:s])
            m["hm"] = hm1
        in_maps.append(m)

    res = run_bass_kernel_spmd(nc, in_maps, core_ids=list(range(8)))
    out = np.empty((B, T, D), res.results[0]["out"].dtype)
    for core in range(8):
        b, h = core // 2, core % 2
        s = h * (NT * TILE)
        out[b, s:s + NT * TILE] = res.results[core]["out"]
    return out
